# version 18
# baseline (speedup 1.0000x reference)
import numpy as np
import concourse.bass as bass
import concourse.mybir as mybir
from concourse.bass_utils import run_bass_kernel_spmd

F32 = mybir.dt.float32
AF = mybir.ActivationFunctionType
ALU = mybir.AluOpType

D = 2048
H = 16
DK = 128
CH = 128
NDC = D // 128


def _wchunk(w):
    d, o = w.shape
    return np.ascontiguousarray(
        w.reshape(d // 128, 128, o // 128, 128).transpose(2, 1, 0, 3))


def _consts():
    idx = np.arange(CH)
    s_p = idx[:, None]
    t_f = idx[None, :]
    same8 = (s_p // 8) == (t_f // 8)
    SC = np.float32(2.0 ** 48)
    mq8 = (same8 & (s_p <= t_f)).astype(np.float32) * SC
    mk8 = (same8 & (s_p < t_f)).astype(np.float32) * SC

    def rectmask(l):
        same2l = (s_p // (2 * l)) == (t_f // (2 * l))
        return (same2l & ((t_f % (2 * l)) >= l)
                & ((s_p % (2 * l)) < l)).astype(np.float32)
    return dict(mq8=mq8, mk8=mk8,
                m16r=rectmask(8).astype(np.uint8), m32r=rectmask(16).astype(np.uint8),
                m64r=rectmask(32).astype(np.uint8), m128r=rectmask(64).astype(np.uint8),
                tri=(s_p <= t_f).astype(np.float32),
                tru=(s_p > t_f).astype(np.float32),
                iden=np.eye(128, dtype=np.float32),
                onec=np.ones((128, 1), np.float32))


class Buf:
    __slots__ = ("w", "rs")
    def __init__(self):
        self.w = None
        self.rs = []


class KB:
    CENG = ("te", "ve", "se", "gp")

    NDSEM = {"sy": 12, "gp": 8}

    def __init__(self, nc):
        self.nc = nc
        self.ops = {e: [] for e in ("te", "ve", "se", "gp", "sy")}
        self.cnt = {e: 0 for e in self.CENG}
        self.dcnt = {"sy": 0, "gp": 0}
        self.sems = {e: nc.alloc_semaphore(f"prog_{e}") for e in self.CENG}
        self.dsems = {q: [nc.alloc_semaphore(f"dma_{q}{i}")
                          for i in range(self.NDSEM[q])]
                      for q in ("sy", "gp")}
        self.waited = {e: {} for e in ("te", "ve", "se", "gp", "sy")}

    def _waits(self, eng, deps):
        need = {}
        for ev in deps:
            if ev is None:
                continue
            k, v = ev
            if self.waited[eng].get(k, 0) >= v:
                continue
            if need.get(k, 0) < v:
                need[k] = v
        outw = []
        for k, v in need.items():
            if k.startswith("e:"):
                sem = self.sems[k[2:]]
            else:
                _, q, slot = k.split(":")
                sem = self.dsems[q][int(slot)]
            outw.append((sem, v))
            self.waited[eng][k] = v
        return outw

    @staticmethod
    def _flat(x, out):
        if x is None:
            return
        if isinstance(x, list):
            for e in x:
                out.append(e)
        else:
            out.append(x)

    def _deps(self, reads, rbufs, wbufs):
        deps = []
        for ev in reads:
            self._flat(ev, deps)
        for b in rbufs:
            self._flat(b.w, deps)
        for b in wbufs:
            self._flat(b.w, deps)
            for e in b.rs:
                self._flat(e, deps)
        return deps

    def op(self, eng, fn, reads=(), rbufs=(), wbufs=()):
        waits = self._waits(eng, self._deps(reads, rbufs, wbufs))
        self.cnt[eng] += 1
        ev = (f"e:{eng}", self.cnt[eng])
        sem = self.sems[eng]

        def emit(h, waits=waits, fn=fn, sem=sem):
            for s, v in waits:
                h.wait_ge(s, v)
            fn(h).then_inc(sem, 1)
        self.ops[eng].append(emit)
        for b in rbufs:
            b.rs.append(ev)
        for b in wbufs:
            b.w = ev
            b.rs = []
        return ev

    def dma(self, q, out_ap, in_ap, reads=(), rbufs=(), wbufs=()):
        n = self.dcnt[q]
        P = self.NDSEM[q]
        slot = n % P
        val = 16 * (n // P + 1)
        key = f"d:{q}:{slot}"
        deps = self._deps(reads, rbufs, wbufs)
        if val > 16:
            deps.append((key, val - 16))
        waits = self._waits(q, deps)
        self.dcnt[q] = n + 1
        ev = (key, val)
        dsem = self.dsems[q][slot]

        def emit(h, waits=waits, out_ap=out_ap, in_ap=in_ap, dsem=dsem):
            for s, v in waits:
                h.wait_ge(s, v)
            h.dma_start(out_ap, in_ap).then_inc(dsem, 16)
        self.ops[q].append(emit)
        for b in rbufs:
            b.rs.append(ev)
        for b in wbufs:
            b.w = ev
            b.rs = []
        return ev

    def fence(self):
        evs = [(f"e:{e}", self.cnt[e]) for e in self.CENG if self.cnt[e] > 0]
        for q in ("sy", "gp"):
            P = self.NDSEM[q]
            for slot in range(min(self.dcnt[q], P)):
                n_slot = (self.dcnt[q] - 1 - slot) // P + 1
                evs.append((f"d:{q}:{slot}", 16 * n_slot))
        return evs

    def finalize(self, final_events):
        nc = self.nc
        waits = self._waits("gp", final_events)
        with nc.Block() as blk:
            @blk.tensor
            def _(h):
                for f in self.ops["te"]:
                    f(h)

            @blk.vector
            def _(h):
                for f in self.ops["ve"]:
                    f(h)

            @blk.scalar
            def _(h):
                for f in self.ops["se"]:
                    f(h)

            @blk.sync
            def _(h):
                for f in self.ops["sy"]:
                    f(h)

            @blk.gpsimd
            def _(h):
                for f in self.ops["gp"]:
                    f(h)
                for s, v in waits:
                    h.wait_ge(s, v)


class Psum:
    def __init__(self, banks):
        self.banks = banks
        self.n = len(banks)
        self.i = 0

    def get(self):
        h, b = self.banks[self.i % self.n]
        self.i += 1
        return h, b


class View:
    __slots__ = ("t", "c0", "w", "b")
    def __init__(self, t, c0, w):
        self.t, self.c0, self.w = t, c0, w
        self.b = Buf()

    def ap(self, a=0, b_=None):
        b_ = self.w if b_ is None else b_
        return self.t.ap()[:, self.c0 + a: self.c0 + b_]


def build_program(Thalf=1024):
    nc = bass.Bass("TRN2", target_bir_lowering=False)
    NCC = Thalf // CH
    RH = Thalf // 2
    NT_R = RH // 128
    SW_R = min(512, RH)
    NSL_R = RH // SW_R
    SW_F = min(256, Thalf)
    NSL_F = Thalf // SW_F
    NTF = SW_F // 128

    dI = {}
    def din(name, shape, dt=F32):
        dI[name] = nc.dram_tensor(name, list(shape), dt, kind="ExternalInput").ap()
        return dI[name]

    x_own = din("x_own", (Thalf, D))
    x_hist = din("x_hist", (Thalf, D))
    wq = din("wq", (NDC, 128, NDC * 128))
    wk = din("wk", (NDC, 128, NDC * 128))
    wv = din("wv", (NDC, 128, NDC * 128))
    wo = din("wo", (D, D))
    wfa = din("wfa", (D, 128))
    wfb = din("wfb", (128, D))
    wb_ = din("wb", (D, 16))
    wga = din("wga", (D, 128))
    wgb = din("wgb", (128, D))
    wgu = din("wgu", (128, 128, NDC * 128))
    wdn = din("wdn", (D * 4, D))
    convq = din("convq", (D, 4))
    convk = din("convk", (D, 4))
    convv = din("convv", (D, 4))
    dtb = din("dtb", (D,))
    alog = din("alog", (128, H))
    normw = din("normw", (128,))
    CNS = _consts()
    cdram = {k: din("c_" + k, v.shape,
                    mybir.dt.uint8 if v.dtype == np.uint8 else F32)
             for k, v in CNS.items()}

    out = nc.dram_tensor("out", [Thalf, D], F32, kind="ExternalOutput").ap()

    sp_k = nc.dram_tensor("sp_k", [2, H, NCC, 128, 128], F32).ap()
    sp_v = nc.dram_tensor("sp_v", [2, H, NCC, 128, 128], F32).ap()
    sp_g = nc.dram_tensor("sp_g", [2, H, NCC, 128, 128], F32).ap()
    sp_q = nc.dram_tensor("sp_q", [H, NCC, 128, 128], F32).ap()
    sp_og = nc.dram_tensor("sp_og", [H, NCC, 128, 128], F32).ap()

    kb = KB(nc)
    banks = [(nc.alloc_psum_tensor(f"pb{i}", [128, 512], F32), Buf())
             for i in range(8)]
    ps_scan = Psum(banks[:4])
    ps_main = Psum(banks[4:])
    ps_gu = Psum(banks[4:6])
    ps_dn = Psum(banks[6:8])
    spk_b = [[[Buf() for _ in range(NCC)] for _ in range(H)] for _ in range(2)]
    spv_b = [[[Buf() for _ in range(NCC)] for _ in range(H)] for _ in range(2)]
    spg_b = [[[Buf() for _ in range(NCC)] for _ in range(H)] for _ in range(2)]
    spq_b = [[Buf() for _ in range(NCC)] for _ in range(H)]
    spog_b = [[Buf() for _ in range(NCC)] for _ in range(H)]
    sbb = [0]

    def sb(name, cols, dt=F32):
        h = nc.alloc_sbuf_tensor(name, [128, cols], dt)
        sbb[0] += cols * (1 if dt == mybir.dt.uint8 else 4)
        return h

    c_mq8 = sb("sc_mq8", 128); c_mk8 = sb("sc_mk8", 128)
    U8 = mybir.dt.uint8
    c_m16 = sb("sc_m16", 128, U8); c_m32 = sb("sc_m32", 128, U8)
    c_m64 = sb("sc_m64", 128, U8); c_m128 = sb("sc_m128", 128, U8)
    c_tri = sb("sc_tri", 128); c_tru = sb("sc_tru", 128)
    c_id = sb("sc_id", 128); c_onec = sb("sc_onec", 1)
    c_eps5 = sb("sc_eps5", 1); c_eps6 = sb("sc_eps6", 1)
    s_wfa = sb("s_wfa", NDC * 128)
    s_wfb = sb("s_wfb", D)
    s_wga = sb("s_wga", NDC * 128)
    s_wgb = sb("s_wgb", D)
    s_wb = sb("s_wb", NDC * 16)
    s_cq = sb("s_cq", NDC * 4); s_ck = sb("s_ck", NDC * 4); s_cv = sb("s_cv", NDC * 4)
    s_dtb = sb("s_dtb", NDC)
    s_al = sb("s_al", H)
    s_nega = sb("s_nega", H)
    s_nw = sb("s_nw", 1)
    cbuf = Buf()

    A1 = sb("A1", 16384)
    AR_BASE = [("ltm", 128), ("lfm", 128), ("e2t", 128), ("knf", 128),
               ("wneg", 128), ("qt", 128), ("rtm", 128), ("ecp", 128),
               ("ebp", 128), ("xk", 128), ("xq", 128), ("yy", 128),
               ("aq", 128), ("akn", 128), ("emp", 128), ("emm", 128),
               ("q6", 128), ("r1", 128), ("r2", 128), ("r3", 128),
               ("r4", 128), ("r5", 128),
               ("cmid", 16), ("rms", 1), ("pcol", 1), ("bneg", 1)]
    ALIAS = {"sgt": "ltm", "z0": "e2t", "z1": "knf", "otm": "ecp",
             "og": "ebp", "q0": "akn", "q1": "emp", "q2": "emm",
             "q3": "xk", "q4": "xq", "q5": "yy", "r0": "lfm"}
    AR_W = sum(w for _, w in AR_BASE)
    A2 = sb("A2", 2 * AR_W + 6 * SW_F)
    arviews = []
    for s in (0, 1):
        off = s * AR_W
        d = {}
        for nm, w in AR_BASE:
            d[nm] = View(A2, off, w)
            off += w
        for a, bnm in ALIAS.items():
            d[a] = d[bnm]
        arviews.append(d)
    fti = [View(A2, 2 * AR_W + i * SW_F, SW_F) for i in range(4)]
    gup = [View(A2, 2 * AR_W + (4 + i) * SW_F, SW_F) for i in range(2)]

    pscr = sb("pscr", 2 * (RH + 3) + 2 * RH)
    pre = [View(pscr, 0, RH + 3), View(pscr, RH + 3, RH + 3)]
    post2 = [View(pscr, 2 * (RH + 3), RH), View(pscr, 2 * (RH + 3) + RH, RH)]
    lnd = sb("lnd", 1024)
    lnd_b = Buf()
    qstg = sb("qstg", 512)
    qstg_b = [Buf(), Buf(), Buf(), Buf()]
    sqdum = sb("sqdum", 128)
    sqdum_b = Buf()

    mid_f = sb("mid_f", Thalf); mid_b = [Buf(), Buf()]
    mid2_f = sb("mid2_f", Thalf); mid2_b = [Buf(), Buf()]
    beta_t = sb("beta_t", 2 * NCC * 16)
    beta_b = [Buf() for _ in range(2 * NCC)]
    xtm = sb("xtm", 2 * D)
    xtm_b = [Buf(), Buf()]
    wdb = sb("wdb", 3 * 2048)
    wdb_b = [Buf(), Buf(), Buf()]
    h1tail = sb("h1tail", NDC * 3); h1tail_b = Buf()
    ttm = sb("ttm", 8 * 128); ttm_b = [Buf() for _ in range(8)]
    lncol = sb("lncol", 16); lncol_b = [Buf(), Buf()]
    tcol = sb("tcol", 16); tcol_b = [Buf(), Buf()]
    S_st = sb("S_st", H * 128); S_b = [Buf() for _ in range(H)]
    ogl = sb("ogl", 2 * 128); ogl_b = [Buf(), Buf()]

    assert sbb[0] * 1.0 <= 207 * 1024, f"SBUF {sbb[0]/1024:.1f}KB over budget"

    ldk = [View(A1, s * 4096, 1024) for s in (0, 1)]
    ldv = [View(A1, s * 4096 + 1024, 1024) for s in (0, 1)]
    ldg = [View(A1, s * 4096 + 2048, 1024) for s in (0, 1)]
    ldq = [View(A1, s * 4096 + 3072, 1024) for s in (0, 1)]
    h1v = [View(A1, 8192 + c * RH, RH) for c in range(NDC)]
    h2f = [View(A1, 8192 + c * SW_F, SW_F) for c in range(NDC)]
    dacc = [View(A1, 8192 + NDC * SW_F + t * D, D) for t in range(NTF)]

    def ve(fn, reads=(), rbufs=(), wbufs=()):
        return kb.op("ve", fn, reads, rbufs, wbufs)

    def se(fn, reads=(), rbufs=(), wbufs=()):
        return kb.op("se", fn, reads, rbufs, wbufs)

    def te(fn, reads=(), rbufs=(), wbufs=()):
        return kb.op("te", fn, reads, rbufs, wbufs)

    def gp(fn, reads=(), rbufs=(), wbufs=()):
        return kb.op("gp", fn, reads, rbufs, wbufs)

    def self_w(b):
        if b.w is None:
            return []
        return b.w if isinstance(b.w, list) else [b.w]

    def stitch(dst_views, src_bufs):
        evs = []
        for b in src_bufs:
            evs += self_w(b) + list(b.rs)
        for v in dst_views:
            v.b.w = None
            v.b.rs = list(evs)

    cev = []
    for dst, key in [(c_mq8, "mq8"), (c_mk8, "mk8"), (c_m16, "m16r"),
                     (c_m32, "m32r"), (c_m64, "m64r"), (c_m128, "m128r"),
                     (c_tri, "tri"), (c_tru, "tru"),
                     (c_id, "iden"), (c_onec, "onec")]:
        cev.append(kb.dma("sy", dst.ap(), cdram[key]))
    cev.append(kb.dma("sy", s_wfb.ap(), wfb))
    cev.append(kb.dma("sy", s_wgb.ap(), wgb))
    for c in range(NDC):
        rr = lambda t: t.rearrange("(c p) f -> c p f", p=128)[c]
        cev.append(kb.dma("sy", s_wfa.ap()[:, c * 128:(c + 1) * 128], rr(wfa)))
        cev.append(kb.dma("sy", s_wga.ap()[:, c * 128:(c + 1) * 128], rr(wga)))
        cev.append(kb.dma("sy", s_wb.ap()[:, c * 16:(c + 1) * 16], rr(wb_)))
        cev.append(kb.dma("sy", s_cq.ap()[:, c * 4:(c + 1) * 4], rr(convq)))
        cev.append(kb.dma("sy", s_ck.ap()[:, c * 4:(c + 1) * 4], rr(convk)))
        cev.append(kb.dma("sy", s_cv.ap()[:, c * 4:(c + 1) * 4], rr(convv)))
        cev.append(kb.dma("sy", s_dtb.ap()[:, c:c + 1],
                   dtb.rearrange("(c p) -> c p", p=128)[c].unsqueeze(1)))
    al_ev = kb.dma("sy", s_al.ap(), alog)
    cev.append(al_ev)
    cev.append(kb.dma("sy", s_nw.ap(), normw.unsqueeze(1)))
    cev.append(ve(lambda h: h.memset(c_eps5.ap(), 1e-5)))
    cev.append(ve(lambda h: h.memset(c_eps6.ap(), 1e-6)))
    e1 = se(lambda h: h.activation(s_nega.ap(), s_al.ap(), AF.Exp), reads=(al_ev,))
    cev.append(ve(lambda h: h.tensor_scalar(s_nega.ap(), s_nega.ap(), -1.0, None,
                                            ALU.mult), reads=(e1,)))
    cbuf.w = cev
    for hh in range(H):
        ve(lambda h, hh=hh: h.memset(S_st.ap()[:, hh * 128:(hh + 1) * 128], 0.0),
           wbufs=(S_b[hh],))
    ve(lambda h: h.memset(h1tail.ap(), 0.0), wbufs=(h1tail_b,))

    def pair_rr(gens):
        gens = list(gens)
        while gens:
            for g in list(gens):
                try:
                    next(g)
                except StopIteration:
                    gens.remove(g)
            yield

    def run_units(*gws):
        gws = [list(gw) for gw in gws]
        while gws:
            for gw in list(gws):
                g, w = gw
                for _ in range(w):
                    try:
                        next(g)
                    except StopIteration:
                        gws.remove(gw)
                        break

    def produce_round_gen(seg, rnd, xdram):
        t0 = rnd * NT_R
        own = seg == 1

        def ln_gen(tcr):
            tc = t0 + tcr
            par = tc % 2
            co = par * 8
            lcb = lncol_b[par]
            xb = xtm_b[par]
            xap = xtm.ap()[:, par * D:(par + 1) * D]
            kb.dma("sy", xap, xdram.rearrange("(a p) d -> a p d", p=128)[tc],
                   wbufs=(xb,))
            ve(lambda h: h.tensor_scalar(
                xap[:, 0:D:2], xap[:, 0:D:2], 1.0, 0.0, ALU.mult, ALU.add,
                accum_out=lncol.ap()[:, co:co + 1]),
               wbufs=(lcb, xb))
            ve(lambda h: h.tensor_scalar(
                xap[:, 1:D:2], xap[:, 1:D:2], 1.0, 0.0, ALU.mult, ALU.add,
                accum_out=lncol.ap()[:, co + 4:co + 5]),
               reads=(lcb.w,), wbufs=(lcb, xb))
            se(lambda h: h.activation(
                lnd.ap(), xap[:, 0:D:2], AF.Square,
                accum_out=lncol.ap()[:, co + 1:co + 2]),
               reads=(lcb.w,), rbufs=(xb,), wbufs=(lcb, lnd_b))
            se(lambda h: h.activation(
                lnd.ap(), xap[:, 1:D:2], AF.Square,
                accum_out=lncol.ap()[:, co + 5:co + 6]),
               reads=(lcb.w,), rbufs=(xb,), wbufs=(lcb, lnd_b))
            ve(lambda h: h.tensor_tensor(lncol.ap()[:, co:co + 2],
                                         lncol.ap()[:, co:co + 2],
                                         lncol.ap()[:, co + 4:co + 6], ALU.add),
               rbufs=(lcb,), wbufs=(lcb,))
            ve(lambda h: h.tensor_scalar(lncol.ap()[:, co + 2:co + 4],
                                         lncol.ap()[:, co:co + 2],
                                         1.0 / D, None, ALU.mult),
               rbufs=(lcb,), wbufs=(lcb,))
            ve(lambda h: h.tensor_tensor(lncol.ap()[:, co + 4:co + 5],
                                         lncol.ap()[:, co + 2:co + 3],
                                         lncol.ap()[:, co + 2:co + 3], ALU.mult),
               rbufs=(lcb,), wbufs=(lcb,))
            ve(lambda h: h.tensor_tensor(lncol.ap()[:, co + 5:co + 6],
                                         lncol.ap()[:, co + 3:co + 4],
                                         lncol.ap()[:, co + 4:co + 5], ALU.subtract),
               rbufs=(lcb,), wbufs=(lcb,))
            se(lambda h: h.activation(lncol.ap()[:, co + 6:co + 7],
                                      lncol.ap()[:, co + 5:co + 6],
                                      AF.Ln, bias=c_eps5.ap()[:, 0:1]),
               rbufs=(lcb, cbuf), wbufs=(lcb,))
            se(lambda h: h.activation(lncol.ap()[:, co + 7:co + 8],
                                      lncol.ap()[:, co + 6:co + 7],
                                      AF.Exp, scale=-0.5),
               rbufs=(lcb,), wbufs=(lcb,))
            ve(lambda h: h.tensor_scalar(
                xap, xap, lncol.ap()[:, co + 2:co + 3], lncol.ap()[:, co + 7:co + 8],
                ALU.subtract, ALU.mult),
               reads=(xb.w,), rbufs=(lcb,), wbufs=(xb,)),
            yield
            for dc in range(NDC):
                pb, pbuf = ps_main.get()
                te(lambda h, pb=pb, dc=dc: h.transpose(
                    pb.ap()[:, 0:128], xap[:, dc * 128:(dc + 1) * 128], c_id.ap()),
                   rbufs=(xb, cbuf), wbufs=(pbuf,))
                dstv = h1v[dc]
                dst = dstv.ap(tcr * 128, (tcr + 1) * 128)
                se(lambda h, pb=pb, dst=dst: h.activation(dst, pb.ap()[:, 0:128],
                                                          AF.Copy),
                   rbufs=(pbuf,), wbufs=(dstv.b,))
                if dc == NDC // 2 - 1:
                    yield

        for a in range(0, NT_R, 2):
            gens = [ln_gen(a)]
            if a + 1 < NT_R:
                gens.append(ln_gen(a + 1))
            yield from pair_rr(gens)

        for sl in range(NSL_R):
            pb, pbuf = ps_main.get()
            for dc in range(NDC):
                te(lambda h, pb=pb, dc=dc, sl=sl: h.matmul(
                    pb.ap()[:, 0:SW_R], s_wfa.ap()[:, dc * 128:(dc + 1) * 128],
                    h1v[dc].ap(sl * SW_R, (sl + 1) * SW_R),
                    start=(dc == 0), stop=(dc == NDC - 1)),
                   reads=(pbuf.w,), rbufs=(h1v[dc].b, cbuf),
                   wbufs=(pbuf,) if dc in (0, NDC - 1) else ())
            a0 = rnd * RH + sl * SW_R
            se(lambda h, pb=pb, a0=a0: h.activation(
                mid_f.ap()[:, a0:a0 + SW_R], pb.ap()[:, 0:SW_R], AF.Copy),
               rbufs=(pbuf,), wbufs=(mid_b[rnd],))
            yield
            if own:
                pb2, pbuf2 = ps_main.get()
                for dc in range(NDC):
                    te(lambda h, pb2=pb2, dc=dc, sl=sl: h.matmul(
                        pb2.ap()[:, 0:SW_R], s_wga.ap()[:, dc * 128:(dc + 1) * 128],
                        h1v[dc].ap(sl * SW_R, (sl + 1) * SW_R),
                        start=(dc == 0), stop=(dc == NDC - 1)),
                       reads=(pbuf2.w,), rbufs=(h1v[dc].b, cbuf),
                       wbufs=(pbuf2,) if dc in (0, NDC - 1) else ())
                se(lambda h, pb2=pb2, a0=a0: h.activation(
                    mid2_f.ap()[:, a0:a0 + SW_R], pb2.ap()[:, 0:SW_R], AF.Copy),
                   rbufs=(pbuf2,), wbufs=(mid2_b[rnd],))
                yield

        for tcr in range(NT_R):
            pb, pbuf = ps_main.get()
            for dc in range(NDC):
                te(lambda h, pb=pb, dc=dc, tcr=tcr: h.matmul(
                    pb.ap()[:, 0:16], h1v[dc].ap(tcr * 128, (tcr + 1) * 128),
                    s_wb.ap()[:, dc * 16:(dc + 1) * 16],
                    start=(dc == 0), stop=(dc == NDC - 1)),
                   reads=(pbuf.w,), rbufs=(h1v[dc].b, cbuf),
                   wbufs=(pbuf,) if dc in (0, NDC - 1) else ())
            bidx = seg * NCC + t0 + tcr
            se(lambda h, pb=pb, bidx=bidx: h.activation(
                beta_t.ap()[:, bidx * 16:(bidx + 1) * 16], pb.ap()[:, 0:16],
                AF.Exp, scale=-1.0),
               rbufs=(pbuf,), wbufs=(beta_b[bidx],))
            ve(lambda h, bidx=bidx: h.tensor_scalar(
                beta_t.ap()[:, bidx * 16:(bidx + 1) * 16],
                beta_t.ap()[:, bidx * 16:(bidx + 1) * 16], 1.0, None, ALU.add),
               reads=(beta_b[bidx].w,), wbufs=(beta_b[bidx],))
            ve(lambda h, bidx=bidx: h.reciprocal(
                beta_t.ap()[:, bidx * 16:(bidx + 1) * 16],
                beta_t.ap()[:, bidx * 16:(bidx + 1) * 16]),
               reads=(beta_b[bidx].w,), wbufs=(beta_b[bidx],))
            if tcr % 2 == 1 or tcr == NT_R - 1:
                yield

        def glog_gen(j):
            par = j % 2
            pon = post2[par]
            for sl in range(NSL_R):
                pb, pbuf = ps_main.get()
                te(lambda h, pb=pb, j=j, sl=sl: h.matmul(
                    pb.ap()[:, 0:SW_R], s_wfb.ap()[:, j * 128:(j + 1) * 128],
                    mid_f.ap()[:, rnd * RH + sl * SW_R:rnd * RH + (sl + 1) * SW_R],
                    start=True, stop=True),
                   rbufs=(mid_b[rnd], cbuf), wbufs=(pbuf,))
                dst = pon.ap(sl * SW_R, (sl + 1) * SW_R)
                se(lambda h, pb=pb, dst=dst, j=j: h.activation(
                    dst, pb.ap()[:, 0:SW_R], AF.Exp, bias=s_dtb.ap()[:, j:j + 1]),
                   rbufs=(pbuf, cbuf), wbufs=(pon.b,))
                se(lambda h, dst=dst: h.activation(dst, dst, AF.Ln,
                                                   bias=c_onec.ap()[:, 0:1]),
                   reads=(pon.b.w,), rbufs=(cbuf,), wbufs=(pon.b,))
                ve(lambda h, dst=dst, j=j: h.tensor_scalar(
                    dst, dst, s_nega.ap()[:, j:j + 1], None, ALU.mult),
                   reads=(pon.b.w,), rbufs=(cbuf,), wbufs=(pon.b,))
            yield
            for tcr in range(NT_R):
                pb, pbuf = ps_main.get()
                te(lambda h, pb=pb, pon=pon, tcr=tcr: h.transpose(
                    pb.ap()[:, 0:128], pon.ap(tcr * 128, (tcr + 1) * 128),
                    c_id.ap()),
                   rbufs=(pon.b, cbuf), wbufs=(pbuf,))
                tslot = par * 4 + tcr
                tb = ttm_b[tslot]
                tap = ttm.ap()[:, tslot * 128:(tslot + 1) * 128]
                se(lambda h, pb=pb, tap=tap: h.activation(tap, pb.ap()[:, 0:128],
                                                          AF.Copy),
                   rbufs=(pbuf,), wbufs=(tb,))
                kb.dma("gp", sp_g[seg, j, t0 + tcr], tap, rbufs=(tb,),
                       wbufs=(spg_b[seg][j][t0 + tcr],))
                if tcr % 2 == 1 or tcr == NT_R - 1:
                    yield
        for a in range(0, NDC, 2):
            yield from pair_rr([glog_gen(a), glog_gen(a + 1)])

        def proj_gen(wdram, conv_sb, spill, spbufs, do_norm, extra_scale, is_q, j):
            par = j % 2
            wb2 = wdb_b[par]
            wap = wdb.ap()[:, par * 2048:(par + 1) * 2048]
            kb.dma("sy", wap, wdram[j], wbufs=(wb2,))
            pv = pre[par]
            pon = post2[par]
            for sl in range(NSL_R):
                pb, pbuf = ps_main.get()
                for dc in range(NDC):
                    te(lambda h, pb=pb, dc=dc, sl=sl, wap=wap: h.matmul(
                        pb.ap()[:, 0:SW_R], wap[:, dc * 128:(dc + 1) * 128],
                        h1v[dc].ap(sl * SW_R, (sl + 1) * SW_R),
                        start=(dc == 0), stop=(dc == NDC - 1)),
                       reads=(pbuf.w,), rbufs=(h1v[dc].b, wb2),
                       wbufs=(pbuf,) if dc in (0, NDC - 1) else ())
                se(lambda h, pb=pb, pv=pv, sl=sl: h.activation(
                    pv.ap(3 + sl * SW_R, 3 + (sl + 1) * SW_R), pb.ap()[:, 0:SW_R],
                    AF.Copy),
                   rbufs=(pbuf,), wbufs=(pv.b,))
            pb, pbuf = ps_main.get()
            for dc in range(NDC):
                te(lambda h, pb=pb, dc=dc, wap=wap: h.matmul(
                    pb.ap()[:, 0:3], wap[:, dc * 128:(dc + 1) * 128],
                    h1tail.ap()[:, dc * 3:(dc + 1) * 3],
                    start=(dc == 0), stop=(dc == NDC - 1)),
                   reads=(pbuf.w,), rbufs=(h1tail_b, wb2),
                   wbufs=(pbuf,) if dc in (0, NDC - 1) else ())
            se(lambda h, pb=pb, pv=pv: h.activation(
                pv.ap(0, 3), pb.ap()[:, 0:3], AF.Copy),
               rbufs=(pbuf,), wbufs=(pv.b,))
            yield
            pp = pon.ap(0, RH)
            ve(lambda h, pv=pv, pp=pp, j=j: h.tensor_scalar(
                pp, pv.ap(0, RH), conv_sb.ap()[:, j * 4:j * 4 + 1],
                None, ALU.mult),
               rbufs=(pv.b, cbuf), wbufs=(pon.b,))
            for d in (1, 2):
                ve(lambda h, pv=pv, pp=pp, j=j, d=d: h.scalar_tensor_tensor(
                    pp, pv.ap(d, d + RH), conv_sb.ap()[:, j * 4 + d:j * 4 + d + 1],
                    pp, ALU.mult, ALU.add),
                   reads=(pon.b.w,), rbufs=(pv.b, cbuf), wbufs=(pon.b,))
            ve(lambda h, pv=pv, pp=pp, j=j: h.scalar_tensor_tensor(
                pp, pv.ap(3, 3 + RH), conv_sb.ap()[:, j * 4 + 3:j * 4 + 4],
                pp, ALU.mult, ALU.add),
               reads=(pon.b.w,), rbufs=(pv.b, cbuf), wbufs=(pon.b,))
            se(lambda h, pp=pp: h.activation(pp, pp, AF.Silu),
               reads=(pon.b.w,), wbufs=(pon.b,))
            yield
            g0 = par * 8
            tcb = tcol_b[par]
            taps = []
            for tcr in range(NT_R):
                pb, pbuf = ps_main.get()
                te(lambda h, pb=pb, pon=pon, tcr=tcr: h.transpose(
                    pb.ap()[:, 0:128], pon.ap(tcr * 128, (tcr + 1) * 128),
                    c_id.ap()),
                   rbufs=(pon.b, cbuf), wbufs=(pbuf,))
                tslot = par * 4 + tcr
                tap = ttm.ap()[:, tslot * 128:(tslot + 1) * 128]
                tb = ttm_b[tslot]
                taps.append((tap, tb))
                if do_norm:
                    se(lambda h, pb=pb, tcr=tcr: h.activation(
                        sqdum.ap(), pb.ap()[:, 0:128],
                        AF.Square, accum_out=tcol.ap()[:, g0 + tcr:g0 + tcr + 1]),
                       reads=(tcb.w,), rbufs=(pbuf,), wbufs=(sqdum_b, tcb))
                se(lambda h, pb=pb, tap=tap: h.activation(
                    tap, pb.ap()[:, 0:128], AF.Copy),
                   rbufs=(pbuf,), wbufs=(tb,))
                if tcr % 2 == 1:
                    yield
            if do_norm:
                se(lambda h: h.activation(tcol.ap()[:, g0 + 4:g0 + 4 + NT_R],
                                          tcol.ap()[:, g0:g0 + NT_R],
                                          AF.Ln, bias=c_eps6.ap()[:, 0:1]),
                   rbufs=(tcb, cbuf), wbufs=(tcb,))
                se(lambda h: h.activation(tcol.ap()[:, g0 + 4:g0 + 4 + NT_R],
                                          tcol.ap()[:, g0 + 4:g0 + 4 + NT_R],
                                          AF.Exp, scale=-0.5),
                   reads=(tcb.w,), wbufs=(tcb,))
                if extra_scale != 1.0:
                    ve(lambda h: h.tensor_scalar(
                        tcol.ap()[:, g0 + 4:g0 + 4 + NT_R],
                        tcol.ap()[:, g0 + 4:g0 + 4 + NT_R],
                        float(extra_scale), None, ALU.mult),
                       reads=(tcb.w,), wbufs=(tcb,))
            for tcr in range(NT_R):
                tap, tb = taps[tcr]
                if do_norm:
                    ve(lambda h, tap=tap, tcr=tcr: h.tensor_scalar(
                        tap, tap, tcol.ap()[:, g0 + 4 + tcr:g0 + 5 + tcr],
                        None, ALU.mult),
                       reads=(tb.w,), rbufs=(tcb,), wbufs=(tb,))
                if is_q:
                    pb2, pbuf2 = ps_main.get()
                    te(lambda h, pb2=pb2, tap=tap: h.transpose(
                        pb2.ap()[:, 0:128], tap, c_id.ap()),
                       rbufs=(tb, cbuf), wbufs=(pbuf2,))
                    qsl = par * 2 + tcr % 2
                    fap = qstg.ap()[:, qsl * 128:(qsl + 1) * 128]
                    qb = qstg_b[qsl]
                    se(lambda h, pb2=pb2, fap=fap: h.activation(
                        fap, pb2.ap()[:, 0:128], AF.Copy),
                       rbufs=(pbuf2,), wbufs=(qb,))
                    kb.dma("gp", spill[j, t0 + tcr], fap, rbufs=(qb,),
                           wbufs=(spq_b[j][t0 + tcr],))
                else:
                    kb.dma("gp", spill[seg, j, t0 + tcr], tap, rbufs=(tb,),
                           wbufs=(spbufs[seg][j][t0 + tcr],))
                if tcr % 2 == 1 or tcr == NT_R - 1:
                    yield

        for wdram, conv_sb, spill, spbufs, do_norm, esc, is_q in (
                (wk, s_ck, sp_k, spk_b, True, 1.0, False),
                (wv, s_cv, sp_v, spv_b, False, 1.0, False),
                (wq, s_cq, sp_q, None, True, float(DK) ** -0.5, True)):
            if is_q and not own:
                continue
            for a in range(0, NDC, 2):
                yield from pair_rr(
                    [proj_gen(wdram, conv_sb, spill, spbufs, do_norm, esc, is_q, a),
                     proj_gen(wdram, conv_sb, spill, spbufs, do_norm, esc, is_q,
                              a + 1)])

        for dc in range(NDC):
            ve(lambda h, dc=dc: h.tensor_scalar(
                h1tail.ap()[:, dc * 3:(dc + 1) * 3], h1v[dc].ap(RH - 3, RH),
                1.0, None, ALU.mult),
               rbufs=(h1v[dc].b,), wbufs=(h1tail_b,))
        yield

    TWO_M24 = float(2.0 ** -24)

    def scan_chunk_gen(seg, cc, hh, slot):
        own = (seg == 1)
        A = arviews[slot]
        def ai(nm):
            return A[nm].ap()
        def ab(nm):
            return A[nm].b
        hb = hh // 8
        lh = hh % 8
        kt = ldk[hb].ap(lh * 128, (lh + 1) * 128)
        vt = ldv[hb].ap(lh * 128, (lh + 1) * 128)
        gt = ldg[hb].ap(lh * 128, (lh + 1) * 128)
        qt_l = ldq[hb].ap(lh * 128, (lh + 1) * 128)
        ktb, vtb, gtb, qtb = ldk[hb].b, ldv[hb].b, ldg[hb].b, ldq[hb].b
        bidx = seg * NCC + cc
        bcol = beta_t.ap()[:, bidx * 16 + hh: bidx * 16 + hh + 1]
        rnd = cc // NT_R

        pbL, pbLb = ps_scan.get()
        te(lambda h: h.matmul(pbL.ap()[:, 0:128], c_tri.ap(), gt,
                              start=True, stop=True),
           rbufs=(cbuf, gtb), wbufs=(pbLb,))
        te(lambda h: h.matmul(pbL.ap()[:, 128:256], c_tru.ap(), gt,
                              start=True, stop=True),
           rbufs=(cbuf, gtb), wbufs=(pbLb,))
        te(lambda h: h.matmul(pbL.ap()[:, 256:257], gt, c_onec.ap(),
                              start=True, stop=True),
           rbufs=(cbuf, gtb), wbufs=(pbLb,))
        se(lambda h: h.activation(ai("ltm"), pbL.ap()[:, 0:128], AF.Copy),
           rbufs=(pbLb,), wbufs=(ab("ltm"),))
        se(lambda h: h.activation(ai("e2t"), pbL.ap()[:, 128:256], AF.Exp),
           rbufs=(pbLb,), wbufs=(ab("e2t"),))
        se(lambda h: h.activation(ai("pcol"), pbL.ap()[:, 256:257], AF.Exp),
           rbufs=(pbLb,), wbufs=(ab("pcol"),))
        pbF, pbFb = ps_scan.get()
        te(lambda h: h.transpose(pbF.ap()[:, 0:128], ai("ltm"), c_id.ap()),
           rbufs=(ab("ltm"), cbuf), wbufs=(pbFb,))
        te(lambda h: h.transpose(pbF.ap()[:, 128:256], kt, c_id.ap()),
           rbufs=(ktb, cbuf), wbufs=(pbFb,))
        se(lambda h: h.activation(ai("lfm"), pbF.ap()[:, 0:128], AF.Copy),
           rbufs=(pbFb,), wbufs=(ab("lfm"),))
        se(lambda h: h.activation(ai("knf"), pbF.ap()[:, 128:256], AF.Copy),
           rbufs=(pbFb,), wbufs=(ab("knf"),))
        lfm = ai("lfm")
        se(lambda h: h.activation(ai("ecp"), lfm, AF.Exp),
           rbufs=(ab("lfm"),), wbufs=(ab("ecp"),))
        ve(lambda h: h.scalar_tensor_tensor(ai("wneg"), ai("ecp"), -1.0, ai("knf"),
                                            ALU.mult, ALU.mult),
           rbufs=(ab("ecp"), ab("knf")), wbufs=(ab("wneg"),))
        if own:
            ve(lambda h: h.tensor_tensor(ai("qt"), qt_l, ai("ecp"), ALU.mult),
               rbufs=(qtb, ab("ecp")), wbufs=(ab("qt"),))
        ve(lambda h: h.tensor_tensor(ai("rtm"), kt, ai("e2t"), ALU.mult),
           rbufs=(ktb, ab("e2t")), wbufs=(ab("rtm"),))
        ve(lambda h: h.tensor_scalar(ai("bneg"), bcol, -1.0, None, ALU.mult),
           rbufs=(beta_b[bidx],), wbufs=(ab("bneg"),))

        yield
        cm = ai("cmid")
        ve(lambda h: h.tensor_scalar(cm, lfm[:, 7:128:8], 0.5, None, ALU.mult),
           rbufs=(ab("lfm"),), wbufs=(ab("cmid"),))
        ve(lambda h: h.scalar_tensor_tensor(cm[:, 1:16], lfm[:, 7:120:8], 0.5,
                                            cm[:, 1:16], ALU.mult, ALU.add),
           reads=(ab("cmid").w,), rbufs=(ab("lfm"),), wbufs=(ab("cmid"),))
        cmb = cm.unsqueeze(2).broadcast_to([128, 16, 8])
        lfm3 = lfm.rearrange("p (a b) -> p a b", b=8)
        ve(lambda h: h.tensor_tensor(ai("emp").rearrange("p (a b) -> p a b", b=8),
                                     lfm3, cmb, ALU.subtract),
           rbufs=(ab("lfm"), ab("cmid")), wbufs=(ab("emp"),))
        se(lambda h: h.activation(ai("emp"), ai("emp"), AF.Exp),
           reads=(ab("emp").w,), wbufs=(ab("emp"),))
        ve(lambda h: h.tensor_tensor(ai("emm").rearrange("p (a b) -> p a b", b=8),
                                     cmb, lfm3, ALU.subtract),
           rbufs=(ab("lfm"), ab("cmid")), wbufs=(ab("emm"),))
        se(lambda h: h.activation(ai("emm"), ai("emm"), AF.Exp),
           reads=(ab("emm").w,), wbufs=(ab("emm"),))
        ve(lambda h: h.scalar_tensor_tensor(ai("xk"), ai("emp"), TWO_M24, ai("knf"),
                                            ALU.mult, ALU.mult),
           rbufs=(ab("emp"), ab("knf")), wbufs=(ab("xk"),))
        ve(lambda h: h.scalar_tensor_tensor(ai("yy"), ai("emm"), TWO_M24, ai("knf"),
                                            ALU.mult, ALU.mult),
           rbufs=(ab("emm"), ab("knf")), wbufs=(ab("yy"),))
        pbMK, pbMKb = ps_scan.get()
        te(lambda h: h.matmul(pbMK.ap()[:, 0:128], ai("yy"), ai("xk"),
                              start=True, stop=True),
           rbufs=(ab("yy"), ab("xk")), wbufs=(pbMKb,))
        if own:
            ve(lambda h: h.scalar_tensor_tensor(ai("xq"), ai("emp"), TWO_M24, qt_l,
                                                ALU.mult, ALU.mult),
               rbufs=(ab("emp"), qtb), wbufs=(ab("xq"),))
            te(lambda h: h.matmul(pbMK.ap()[:, 128:256], ai("yy"), ai("xq"),
                                  start=True, stop=True),
               rbufs=(ab("yy"), ab("xq")), wbufs=(pbMKb,))
        ve(lambda h: h.tensor_tensor(ai("akn"), pbMK.ap()[:, 0:128], c_mk8.ap(),
                                     ALU.mult),
           rbufs=(pbMKb, cbuf), wbufs=(ab("akn"),))
        if own:
            ve(lambda h: h.tensor_tensor(ai("aq"), pbMK.ap()[:, 128:256], c_mq8.ap(),
                                         ALU.mult),
               rbufs=(pbMKb, cbuf), wbufs=(ab("aq"),))

        def build_cp(l):
            nb = 128 // l
            ve(lambda h, l=l: h.tensor_scalar(ai("ecp")[:, 0:l], lfm[:, 0:l],
                                              1.0, None, ALU.mult),
               rbufs=(ab("lfm"),), wbufs=(ab("ecp"),))
            ve(lambda h, l=l, nb=nb: h.tensor_tensor(
                ai("ecp")[:, l:128].rearrange("p (a b) -> p a b", b=l),
                lfm[:, l:128].rearrange("p (a b) -> p a b", b=l),
                lfm[:, l - 1:128 - l:l].unsqueeze(2).broadcast_to([128, nb - 1, l]),
                ALU.subtract),
               reads=(ab("ecp").w,), rbufs=(ab("lfm"),), wbufs=(ab("ecp"),))
            se(lambda h: h.activation(ai("ecp"), ai("ecp"), AF.Exp),
               reads=(ab("ecp").w,), wbufs=(ab("ecp"),))
            ve(lambda h, l=l, nb=nb: h.tensor_tensor(
                ai("ebp").rearrange("p (a b) -> p a b", b=l),
                lfm[:, l - 1:128:l].unsqueeze(2).broadcast_to([128, nb, l]),
                lfm.rearrange("p (a b) -> p a b", b=l),
                ALU.subtract),
               rbufs=(ab("lfm"),), wbufs=(ab("ebp"),))
            se(lambda h: h.activation(ai("ebp"), ai("ebp"), AF.Exp),
               reads=(ab("ebp").w,), wbufs=(ab("ebp"),))
            gp(lambda h: h.tensor_tensor(ai("xk"), ai("knf"), ai("ecp"), ALU.mult),
               rbufs=(ab("knf"), ab("ecp")), wbufs=(ab("xk"),))
            ve(lambda h: h.tensor_tensor(ai("yy"), ai("knf"), ai("ebp"), ALU.mult),
               rbufs=(ab("knf"), ab("ebp")), wbufs=(ab("yy"),))
            if own:
                gp(lambda h: h.tensor_tensor(ai("xq"), qt_l, ai("ecp"), ALU.mult),
                   rbufs=(qtb, ab("ecp")), wbufs=(ab("xq"),))

        yield
        for l, mask_c in ((8, c_m16), (16, c_m32), (32, c_m64), (64, c_m128)):
            yield
            build_cp(l)
            pbK2, pbK2b = ps_scan.get()
            te(lambda h, pbK2=pbK2: h.matmul(pbK2.ap()[:, 0:128], ai("yy"), ai("xk"),
                                             start=True, stop=True),
               rbufs=(ab("yy"), ab("xk")), wbufs=(pbK2b,))
            if own:
                te(lambda h, pbK2=pbK2: h.matmul(pbK2.ap()[:, 128:256], ai("yy"),
                                                 ai("xq"), start=True, stop=True),
                   rbufs=(ab("yy"), ab("xq")), wbufs=(pbK2b,))
            ve(lambda h, pbK2=pbK2, mask_c=mask_c: h.copy_predicated(
                ai("akn"), mask_c.ap(), pbK2.ap()[:, 0:128]),
               reads=(ab("akn").w,), rbufs=(pbK2b, cbuf), wbufs=(ab("akn"),))
            if own:
                ve(lambda h, pbK2=pbK2, mask_c=mask_c: h.copy_predicated(
                    ai("aq"), mask_c.ap(), pbK2.ap()[:, 128:256]),
                   reads=(ab("aq").w,), rbufs=(pbK2b, cbuf), wbufs=(ab("aq"),))

        ve(lambda h: h.tensor_scalar(ai("akn"), ai("akn"), ai("bneg"), None,
                                     ALU.mult),
           reads=(ab("akn").w,), rbufs=(ab("bneg"),), wbufs=(ab("akn"),))

        yield
        qn = [f"q{i}" for i in range(7)]
        rn = [f"r{i}" for i in range(6)]
        pbT, pbTb = ps_scan.get()
        te(lambda h: h.transpose(pbT.ap()[:, 0:128], ai("q0"), c_id.ap()),
           rbufs=(ab("q0"), cbuf), wbufs=(pbTb,))
        se(lambda h: h.activation(ai("r0"), pbT.ap()[:, 0:128], AF.Copy),
           rbufs=(pbTb,), wbufs=(ab("r0"),))
        for i in range(6):
            yield
            pbA, pbAb = ps_scan.get()
            te(lambda h, i=i, pbA=pbA: h.matmul(pbA.ap()[:, 0:128], ai(rn[i]),
                                                ai(qn[i]), start=True, stop=True),
               rbufs=(ab(rn[i]), ab(qn[i])), wbufs=(pbAb,))
            if i < 5:
                te(lambda h, i=i, pbA=pbA: h.matmul(pbA.ap()[:, 128:256], ai(qn[i]),
                                                    ai(rn[i]), start=True, stop=True),
                   rbufs=(ab(qn[i]), ab(rn[i])), wbufs=(pbAb,))
            se(lambda h, i=i, pbA=pbA: h.activation(ai(qn[i + 1]),
                                                    pbA.ap()[:, 0:128], AF.Copy),
               rbufs=(pbAb,), wbufs=(ab(qn[i + 1]),))
            if i < 5:
                se(lambda h, i=i, pbA=pbA: h.activation(ai(rn[i + 1]),
                                                        pbA.ap()[:, 128:256], AF.Copy),
                   rbufs=(pbAb,), wbufs=(ab(rn[i + 1]),))

        yield
        Sap = S_st.ap()[:, hh * 128:(hh + 1) * 128]
        pbW, pbWb = ps_scan.get()
        te(lambda h: h.matmul(pbW.ap()[:, 0:128], ai("wneg"), Sap,
                              start=True, stop=True),
           rbufs=(ab("wneg"), S_b[hh]), wbufs=(pbWb,))
        ve(lambda h: h.tensor_add(ai("z0"), vt, pbW.ap()[:, 0:128]),
           rbufs=(vtb, pbWb), wbufs=(ab("z0"),))
        zs = ["z0", "z1"]
        cur = 0
        for i in range(6, -1, -1):
            yield
            pbZ, pbZb = ps_scan.get()
            te(lambda h, i=i, pbZ=pbZ, cur=cur: h.matmul(
                pbZ.ap()[:, 0:128], ai(qn[i]), ai(zs[cur]), start=True, stop=True),
               rbufs=(ab(qn[i]), ab(zs[cur])), wbufs=(pbZb,))
            ve(lambda h, pbZ=pbZ, cur=cur: h.tensor_add(
                ai(zs[1 - cur]), ai(zs[cur]), pbZ.ap()[:, 0:128]),
               rbufs=(ab(zs[cur]), pbZb), wbufs=(ab(zs[1 - cur]),))
            cur = 1 - cur
        u_nm = zs[1 - cur]
        ve(lambda h: h.tensor_scalar(ai(u_nm), ai(zs[cur]), bcol, None,
                                     ALU.mult),
           rbufs=(ab(zs[cur]), beta_b[bidx]), wbufs=(ab(u_nm),))

        yield
        if own:
            pbO, pbOb = ps_scan.get()
            te(lambda h: h.matmul(pbO.ap()[:, 0:128], ai("qt"), Sap,
                                  start=True, stop=False),
               rbufs=(ab("qt"), S_b[hh]), wbufs=(pbOb,))
            e_o = te(lambda h: h.matmul(pbO.ap()[:, 0:128], ai("aq"), ai(u_nm),
                                        start=False, stop=True),
                     reads=(pbOb.w,), rbufs=(ab("aq"), ab(u_nm)))
            pbOb.w = e_o
            se(lambda h: h.activation(ai("otm"), pbO.ap()[:, 0:128], AF.Copy),
               rbufs=(pbOb,), wbufs=(ab("otm"),))
            se(lambda h: h.activation(ai("og"), pbO.ap()[:, 0:128], AF.Square,
                                      accum_out=cm[:, 0:1]),
               reads=(ab("cmid").w,), rbufs=(pbOb,), wbufs=(ab("og"), ab("cmid")))
            se(lambda h: h.activation(cm[:, 1:2], cm[:, 0:1],
                                      AF.Ln, bias=c_eps6.ap()[:, 0:1],
                                      scale=1.0 / 128.0),
               rbufs=(ab("cmid"), cbuf), wbufs=(ab("cmid"),))
            se(lambda h: h.activation(ai("rms"), cm[:, 1:2], AF.Exp, scale=-0.5),
               rbufs=(ab("cmid"),), wbufs=(ab("rms"),))
            pbG, pbGb = ps_scan.get()
            te(lambda h: h.matmul(pbG.ap()[:, 0:128],
                                  mid2_f.ap()[:, cc * 128:(cc + 1) * 128],
                                  s_wgb.ap()[:, hh * 128:(hh + 1) * 128],
                                  start=True, stop=True),
               rbufs=(mid2_b[rnd], cbuf), wbufs=(pbGb,))
            se(lambda h: h.activation(ai("sgt"), pbG.ap()[:, 0:128], AF.Exp,
                                      scale=-1.0),
               rbufs=(pbGb,), wbufs=(ab("sgt"),))
            ve(lambda h: h.tensor_scalar(ai("sgt"), ai("sgt"), 1.0, None, ALU.add),
               reads=(ab("sgt").w,), wbufs=(ab("sgt"),))
            ve(lambda h: h.reciprocal(ai("sgt"), ai("sgt")),
               reads=(ab("sgt").w,), wbufs=(ab("sgt"),))
            ve(lambda h: h.tensor_scalar(ai("og"), ai("otm"), ai("rms"), None,
                                         ALU.mult),
               rbufs=(ab("otm"), ab("rms")), wbufs=(ab("og"),))
            ve(lambda h: h.tensor_tensor(ai("og"), ai("og"), ai("sgt"), ALU.mult),
               reads=(ab("og").w,), rbufs=(ab("sgt"),), wbufs=(ab("og"),))
            pbH, pbHb = ps_scan.get()
            te(lambda h: h.transpose(pbH.ap()[:, 0:128], ai("og"), c_id.ap()),
               rbufs=(ab("og"), cbuf), wbufs=(pbHb,))
            se(lambda h: h.activation(ai("otm"), pbH.ap()[:, 0:128], AF.Copy,
                                      scale=s_nw.ap()[:, 0:1]),
               rbufs=(pbHb, cbuf), wbufs=(ab("otm"),))
            kb.dma("gp", sp_og[hh, cc], ai("otm"), rbufs=(ab("otm"),),
                   wbufs=(spog_b[hh][cc],))

        pbS, pbSb = ps_scan.get()
        te(lambda h: h.matmul(pbS.ap()[:, 0:128], ai("rtm"), ai(u_nm),
                              start=True, stop=True),
           rbufs=(ab("rtm"), ab(u_nm)), wbufs=(pbSb,))
        ve(lambda h: h.scalar_tensor_tensor(Sap, Sap, ai("pcol"),
                                            pbS.ap()[:, 0:128], ALU.mult, ALU.add),
           reads=(S_b[hh].w,) + tuple(S_b[hh].rs), rbufs=(pbSb, ab("pcol")),
           wbufs=(S_b[hh],))

    def scan_chunks_gen(seg, c_lo, c_hi, done=None):
        for cc in range(c_lo, c_hi):
            for hb in (0, 1):
                for tens, spl, sbv in ((ldk, sp_k, spk_b), (ldv, sp_v, spv_b),
                                       (ldg, sp_g, spg_b)):
                    vw = tens[hb]
                    evs = []
                    for i in range(8):
                        hh = hb * 8 + i
                        ev = kb.dma("sy", vw.ap(i * 128, (i + 1) * 128),
                                    spl[seg, hh, cc],
                                    rbufs=(sbv[seg][hh][cc],),
                                    reads=((tuple(self_w(vw.b)) + tuple(vw.b.rs))
                                           if i == 0 else ()))
                        evs.append(ev)
                    vw.b.w = list(evs)
                    vw.b.rs = []
                if seg == 1:
                    vw = ldq[hb]
                    evs = []
                    for i in range(8):
                        hh = hb * 8 + i
                        ev = kb.dma("sy", vw.ap(i * 128, (i + 1) * 128),
                                    sp_q[hh, cc],
                                    rbufs=(spq_b[hh][cc],),
                                    reads=((tuple(self_w(vw.b)) + tuple(vw.b.rs))
                                           if i == 0 else ()))
                        evs.append(ev)
                    vw.b.w = list(evs)
                    vw.b.rs = []
                yield
                for p0 in range(hb * 8, hb * 8 + 8, 2):
                    yield from pair_rr([scan_chunk_gen(seg, cc, p0, 0),
                                        scan_chunk_gen(seg, cc, p0 + 1, 1)])
            if done is not None:
                done[0] = cc + 1

    def wo_tc_gen(tc):
        bks = [ps_main.get() for _ in range(4)]
        for hh in range(H):
            wslot = hh % 2
            wap = wdb.ap()[:, wslot * 2048:(wslot + 1) * 2048]
            wbuf = wdb_b[wslot]
            kb.dma("sy", wap, wo.rearrange("(a p) d -> a p d", p=128)[hh],
                   wbufs=(wbuf,))
            oslot = hh % 2
            oap = ogl.ap()[:, oslot * 128:(oslot + 1) * 128]
            obuf = ogl_b[oslot]
            kb.dma("sy", oap, sp_og[hh, tc], rbufs=(spog_b[hh][tc],),
                   wbufs=(obuf,))
            for dsl in range(4):
                pb, pbuf = bks[dsl]
                te(lambda h, pb=pb, oap=oap, wap=wap, dsl=dsl, hh=hh: h.matmul(
                    pb.ap()[:, 0:512], oap,
                    wap[:, dsl * 512:(dsl + 1) * 512],
                    start=(hh == 0), stop=(hh == H - 1)),
                   reads=(pbuf.w,), rbufs=(obuf, wbuf),
                   wbufs=(pbuf,) if hh in (0, H - 1) else ())
            if hh % 2 == 1:
                yield
        xslot = tc % 2
        xap = xtm.ap()[:, xslot * D:(xslot + 1) * D]
        xbuf = xtm_b[xslot]
        kb.dma("sy", xap, x_own.rearrange("(a p) d -> a p d", p=128)[tc],
               wbufs=(xbuf,))
        for dsl in range(4):
            pb, pbuf = bks[dsl]
            ve(lambda h, pb=pb, xap=xap, dsl=dsl: h.tensor_tensor(
                xap[:, dsl * 512:(dsl + 1) * 512],
                xap[:, dsl * 512:(dsl + 1) * 512],
                pb.ap()[:, 0:512], ALU.add),
               reads=(xbuf.w,), rbufs=(pbuf,), wbufs=(xbuf,))
        yield

    def ffn_slice_gen(sl):
        for tci in range(NTF):
            tc = sl * NTF + tci
            par = tc % 2
            co = par * 8
            lcb = lncol_b[par]
            xap = xtm.ap()[:, par * D:(par + 1) * D]
            xbuf = xtm_b[par]
            ve(lambda h, xap=xap, co=co: h.tensor_scalar(
                xap[:, 0:D:2], xap[:, 0:D:2], 1.0, 0.0, ALU.mult, ALU.add,
                accum_out=lncol.ap()[:, co:co + 1]),
               wbufs=(lcb, xbuf))
            ve(lambda h, xap=xap, co=co: h.tensor_scalar(
                xap[:, 1:D:2], xap[:, 1:D:2], 1.0, 0.0, ALU.mult, ALU.add,
                accum_out=lncol.ap()[:, co + 4:co + 5]),
               reads=(lcb.w,), wbufs=(lcb, xbuf))
            se(lambda h, xap=xap, co=co: h.activation(
                lnd.ap(), xap[:, 0:D:2], AF.Square,
                accum_out=lncol.ap()[:, co + 1:co + 2]),
               reads=(lcb.w,), rbufs=(xbuf,), wbufs=(lcb, lnd_b))
            se(lambda h, xap=xap, co=co: h.activation(
                lnd.ap(), xap[:, 1:D:2], AF.Square,
                accum_out=lncol.ap()[:, co + 5:co + 6]),
               reads=(lcb.w,), rbufs=(xbuf,), wbufs=(lcb, lnd_b))
            ve(lambda h, co=co: h.tensor_tensor(lncol.ap()[:, co:co + 2],
                                                lncol.ap()[:, co:co + 2],
                                                lncol.ap()[:, co + 4:co + 6],
                                                ALU.add),
               rbufs=(lcb,), wbufs=(lcb,))
            ve(lambda h, co=co: h.tensor_scalar(lncol.ap()[:, co + 2:co + 4],
                                                lncol.ap()[:, co:co + 2],
                                                1.0 / D, None, ALU.mult),
               rbufs=(lcb,), wbufs=(lcb,))
            ve(lambda h, co=co: h.tensor_tensor(lncol.ap()[:, co + 4:co + 5],
                                                lncol.ap()[:, co + 2:co + 3],
                                                lncol.ap()[:, co + 2:co + 3],
                                                ALU.mult),
               rbufs=(lcb,), wbufs=(lcb,))
            ve(lambda h, co=co: h.tensor_tensor(lncol.ap()[:, co + 5:co + 6],
                                                lncol.ap()[:, co + 3:co + 4],
                                                lncol.ap()[:, co + 4:co + 5],
                                                ALU.subtract),
               rbufs=(lcb,), wbufs=(lcb,))
            se(lambda h, co=co: h.activation(lncol.ap()[:, co + 6:co + 7],
                                             lncol.ap()[:, co + 5:co + 6],
                                             AF.Ln, bias=c_eps5.ap()[:, 0:1]),
               rbufs=(lcb, cbuf), wbufs=(lcb,))
            se(lambda h, co=co: h.activation(lncol.ap()[:, co + 7:co + 8],
                                             lncol.ap()[:, co + 6:co + 7],
                                             AF.Exp, scale=-0.5),
               rbufs=(lcb,), wbufs=(lcb,))
            ve(lambda h, xap=xap, tci=tci: h.tensor_scalar(
                dacc[tci].ap(), xap, 1.0, None, ALU.mult),
               rbufs=(xbuf,), wbufs=(dacc[tci].b,))
            ve(lambda h, xap=xap, co=co: h.tensor_scalar(
                xap, xap, lncol.ap()[:, co + 2:co + 3],
                lncol.ap()[:, co + 7:co + 8], ALU.subtract, ALU.mult),
               reads=(xbuf.w,), rbufs=(lcb,), wbufs=(xbuf,))
            yield
            for dc in range(NDC):
                pb, pbuf = ps_main.get()
                te(lambda h, pb=pb, dc=dc, xap=xap: h.transpose(
                    pb.ap()[:, 0:128], xap[:, dc * 128:(dc + 1) * 128],
                    c_id.ap()),
                   rbufs=(xbuf, cbuf), wbufs=(pbuf,))
                dst = h2f[dc].ap(tci * 128, (tci + 1) * 128)
                se(lambda h, pb=pb, dst=dst: h.activation(
                    dst, pb.ap()[:, 0:128], AF.Copy),
                   rbufs=(pbuf,), wbufs=(h2f[dc].b,))
                if dc == NDC // 2 - 1:
                    yield
            yield
        wap3 = wdb.ap()[:, 2 * 2048:3 * 2048]
        wbuf3 = wdb_b[2]

        def gu_emit(ip):
            for which, ic in ((0, ip), (1, ip + 64)):
                wap = wdb.ap()[:, which * 2048:(which + 1) * 2048]
                wbuf = wdb_b[which]
                kb.dma("sy", wap, wgu[ic], wbufs=(wbuf,))
                pb, pbuf = ps_gu.get()
                for dc in range(NDC):
                    te(lambda h, pb=pb, wap=wap, dc=dc: h.matmul(
                        pb.ap()[:, 0:SW_F], wap[:, dc * 128:(dc + 1) * 128],
                        h2f[dc].ap(),
                        start=(dc == 0), stop=(dc == NDC - 1)),
                       reads=(pbuf.w,), rbufs=(h2f[dc].b, wbuf),
                       wbufs=(pbuf,) if dc in (0, NDC - 1) else ())
                gv = gup[which]
                if which == 0:
                    se(lambda h, pb=pb, gv=gv: h.activation(
                        gv.ap(), pb.ap()[:, 0:SW_F], AF.Silu),
                       rbufs=(pbuf,), wbufs=(gv.b,))
                else:
                    se(lambda h, pb=pb, gv=gv: h.activation(
                        gv.ap(), pb.ap()[:, 0:SW_F], AF.Copy),
                       rbufs=(pbuf,), wbufs=(gv.b,))
            ve(lambda h, u=ip % 4: h.tensor_tensor(
                fti[u].ap(), gup[0].ap(), gup[1].ap(), ALU.mult),
               rbufs=(gup[0].b, gup[1].b), wbufs=(fti[ip % 4].b,))

        def down_emit(ip):
            u = ip % 4
            for tci in range(NTF):
                for dsl in range(4):
                    pb, pbuf = ps_dn.get()
                    te(lambda h, pb=pb, u=u, tci=tci, dsl=dsl:
                       h.matmul(
                           pb.ap()[:, 0:512],
                           fti[u].ap(tci * 128, (tci + 1) * 128),
                           wap3[:, dsl * 512:(dsl + 1) * 512],
                           start=True, stop=True),
                       rbufs=(fti[u].b, wbuf3), wbufs=(pbuf,))
                    ve(lambda h, pb=pb, tci=tci, dsl=dsl: h.tensor_tensor(
                        dacc[tci].ap(dsl * 512, (dsl + 1) * 512),
                        dacc[tci].ap(dsl * 512, (dsl + 1) * 512),
                        pb.ap()[:, 0:512], ALU.add),
                       reads=(dacc[tci].b.w,), rbufs=(pbuf,),
                       wbufs=(dacc[tci].b,))

        prev = None
        for ip in range(64):
            if prev is not None:
                kb.dma("sy", wap3,
                       wdn.rearrange("(a p) d -> a p d", p=128)[prev],
                       wbufs=(wbuf3,))
            gu_emit(ip)
            yield
            if prev is not None:
                down_emit(prev)
                yield
            prev = ip
        kb.dma("sy", wap3, wdn.rearrange("(a p) d -> a p d", p=128)[prev],
               wbufs=(wbuf3,))
        down_emit(prev)
        yield
        for tci in range(NTF):
            tc = sl * NTF + tci
            ev = kb.dma("gp", out.rearrange("(a p) d -> a p d", p=128)[tc],
                        dacc[tci].ap(), rbufs=(dacc[tci].b,))
            final_evs.append(ev)
        yield

    final_evs = []
    own_done = [0]

    def wf_gen():
        stitch(h2f + dacc, [v.b for v in h1v])
        for sl in range(NSL_F):
            for tci in range(NTF):
                tc = sl * NTF + tci
                while own_done[0] < tc + 1:
                    yield
                yield from wo_tc_gen(tc)
            yield from ffn_slice_gen(sl)

    HC = NCC // 2
    run_units((produce_round_gen(0, 0, x_hist), 1))
    run_units((produce_round_gen(0, 1, x_hist), 1), (scan_chunks_gen(0, 0, HC), 4))
    run_units((produce_round_gen(1, 0, x_own), 1), (scan_chunks_gen(0, HC, NCC), 4))
    run_units((produce_round_gen(1, 1, x_own), 1), (scan_chunks_gen(1, 0, HC), 4))
    own_done[0] = HC
    run_units((wf_gen(), 2), (scan_chunks_gen(1, HC, NCC, own_done), 1))
    kb.finalize(final_evs)
    return nc


def kernel(L0_attn_hidden_states, L0_attn_Wq, L0_attn_Wk, L0_attn_Wv,
           L0_attn_conv_q, L0_attn_conv_k, L0_attn_conv_v, L0_attn_A_log,
           L0_attn_Wfa, L0_attn_Wfb, L0_attn_dt_bias, L0_attn_Wb,
           L0_attn_Wga, L0_attn_Wgb, L0_attn_norm_w, L0_attn_Wo,
           L0_ffn_w_gate_up, L0_ffn_w_down):
    X = np.ascontiguousarray(np.asarray(L0_attn_hidden_states, np.float32))
    B, T, _ = X.shape
    Thalf = T // 2
    consts = _consts()
    f32 = lambda a: np.ascontiguousarray(np.asarray(a, np.float32))
    wq_c = _wchunk(f32(L0_attn_Wq)).reshape(NDC, 128, NDC * 128)
    wk_c = _wchunk(f32(L0_attn_Wk)).reshape(NDC, 128, NDC * 128)
    wv_c = _wchunk(f32(L0_attn_Wv)).reshape(NDC, 128, NDC * 128)
    wgu_c = _wchunk(f32(np.asarray(L0_ffn_w_gate_up).T)).reshape(128, 128, NDC * 128)
    wdn_t = f32(np.asarray(L0_ffn_w_down).T)
    alog_rep = np.ascontiguousarray(
        np.broadcast_to(f32(L0_attn_A_log).reshape(1, H), (128, H)))

    nc = build_program(Thalf)
    common = dict(
        wq=wq_c, wk=wk_c, wv=wv_c, wo=f32(L0_attn_Wo),
        wfa=f32(L0_attn_Wfa), wfb=f32(L0_attn_Wfb), wb=f32(L0_attn_Wb),
        wga=f32(L0_attn_Wga), wgb=f32(L0_attn_Wgb), wgu=wgu_c, wdn=wdn_t,
        convq=f32(L0_attn_conv_q), convk=f32(L0_attn_conv_k),
        convv=f32(L0_attn_conv_v), dtb=f32(L0_attn_dt_bias), alog=alog_rep,
        normw=f32(L0_attn_norm_w),
    )
    for k, v in consts.items():
        common["c_" + k] = v
    zeros = np.zeros((Thalf, D), np.float32)
    in_maps = []
    for core in range(8):
        b, half = core // 2, core % 2
        m = dict(common)
        m["x_own"] = np.ascontiguousarray(X[b, half * Thalf:(half + 1) * Thalf])
        m["x_hist"] = (np.ascontiguousarray(X[b, 0:Thalf]) if half == 1
                       else zeros)
        in_maps.append(m)
    res = run_bass_kernel_spmd(nc, in_maps, core_ids=list(range(8)))
    outp = np.empty((B, T, D), np.float32)
    for core in range(8):
        b, half = core // 2, core % 2
        outp[b, half * Thalf:(half + 1) * Thalf] = res.results[core]["out"]
    return outp


# version 19
# speedup vs baseline: 1.1423x; 1.1423x over previous
import numpy as np
import concourse.bass as bass
import concourse.mybir as mybir
from concourse.bass_utils import run_bass_kernel_spmd

F32 = mybir.dt.float32
AF = mybir.ActivationFunctionType
ALU = mybir.AluOpType

D = 2048
H = 16
DK = 128
CH = 128
NDC = D // 128


def _wchunk(w):
    d, o = w.shape
    return np.ascontiguousarray(
        w.reshape(d // 128, 128, o // 128, 128).transpose(2, 1, 0, 3))


def _consts():
    idx = np.arange(CH)
    s_p = idx[:, None]
    t_f = idx[None, :]
    same8 = (s_p // 8) == (t_f // 8)
    SC = np.float32(2.0 ** 48)
    mq8 = (same8 & (s_p <= t_f)).astype(np.float32) * SC
    mk8 = (same8 & (s_p < t_f)).astype(np.float32) * SC

    def rectmask(l):
        same2l = (s_p // (2 * l)) == (t_f // (2 * l))
        return (same2l & ((t_f % (2 * l)) >= l)
                & ((s_p % (2 * l)) < l)).astype(np.float32)
    return dict(mq8=mq8, mk8=mk8,
                m16r=rectmask(8).astype(np.uint8), m32r=rectmask(16).astype(np.uint8),
                m64r=rectmask(32).astype(np.uint8), m128r=rectmask(64).astype(np.uint8),
                tri=(s_p <= t_f).astype(np.float32),
                tru=(s_p > t_f).astype(np.float32),
                iden=np.eye(128, dtype=np.float32),
                onec=np.ones((128, 1), np.float32))


class Buf:
    __slots__ = ("w", "rs")
    def __init__(self):
        self.w = None
        self.rs = []


class KB:
    CENG = ("te", "ve", "se", "gp")

    NDSEM = {"sy": 12, "gp": 8}

    def __init__(self, nc):
        self.nc = nc
        self.ops = {e: [] for e in ("te", "ve", "se", "gp", "sy")}
        self.cnt = {e: 0 for e in self.CENG}
        self.dcnt = {"sy": 0, "gp": 0}
        self.sems = {e: nc.alloc_semaphore(f"prog_{e}") for e in self.CENG}
        self.dsems = {q: [nc.alloc_semaphore(f"dma_{q}{i}")
                          for i in range(self.NDSEM[q])]
                      for q in ("sy", "gp")}
        self.waited = {e: {} for e in ("te", "ve", "se", "gp", "sy")}

    def _waits(self, eng, deps):
        need = {}
        for ev in deps:
            if ev is None:
                continue
            k, v = ev
            if self.waited[eng].get(k, 0) >= v:
                continue
            if need.get(k, 0) < v:
                need[k] = v
        outw = []
        for k, v in need.items():
            if k.startswith("e:"):
                sem = self.sems[k[2:]]
            else:
                _, q, slot = k.split(":")
                sem = self.dsems[q][int(slot)]
            outw.append((sem, v))
            self.waited[eng][k] = v
        return outw

    @staticmethod
    def _flat(x, out):
        if x is None:
            return
        if isinstance(x, list):
            for e in x:
                out.append(e)
        else:
            out.append(x)

    def _deps(self, reads, rbufs, wbufs):
        deps = []
        for ev in reads:
            self._flat(ev, deps)
        for b in rbufs:
            self._flat(b.w, deps)
        for b in wbufs:
            self._flat(b.w, deps)
            for e in b.rs:
                self._flat(e, deps)
        return deps

    def op(self, eng, fn, reads=(), rbufs=(), wbufs=()):
        waits = self._waits(eng, self._deps(reads, rbufs, wbufs))
        self.cnt[eng] += 1
        ev = (f"e:{eng}", self.cnt[eng])
        sem = self.sems[eng]

        def emit(h, waits=waits, fn=fn, sem=sem):
            for s, v in waits:
                h.wait_ge(s, v)
            fn(h).then_inc(sem, 1)
        self.ops[eng].append(emit)
        for b in rbufs:
            b.rs.append(ev)
        for b in wbufs:
            b.w = ev
            b.rs = []
        return ev

    def dma(self, q, out_ap, in_ap, reads=(), rbufs=(), wbufs=()):
        n = self.dcnt[q]
        P = self.NDSEM[q]
        slot = n % P
        val = 16 * (n // P + 1)
        key = f"d:{q}:{slot}"
        deps = self._deps(reads, rbufs, wbufs)
        if val > 16:
            deps.append((key, val - 16))
        waits = self._waits(q, deps)
        self.dcnt[q] = n + 1
        ev = (key, val)
        dsem = self.dsems[q][slot]

        def emit(h, waits=waits, out_ap=out_ap, in_ap=in_ap, dsem=dsem):
            for s, v in waits:
                h.wait_ge(s, v)
            h.dma_start(out_ap, in_ap).then_inc(dsem, 16)
        self.ops[q].append(emit)
        for b in rbufs:
            b.rs.append(ev)
        for b in wbufs:
            b.w = ev
            b.rs = []
        return ev

    def fence(self):
        evs = [(f"e:{e}", self.cnt[e]) for e in self.CENG if self.cnt[e] > 0]
        for q in ("sy", "gp"):
            P = self.NDSEM[q]
            for slot in range(min(self.dcnt[q], P)):
                n_slot = (self.dcnt[q] - 1 - slot) // P + 1
                evs.append((f"d:{q}:{slot}", 16 * n_slot))
        return evs

    def finalize(self, final_events):
        nc = self.nc
        waits = self._waits("gp", final_events)
        with nc.Block() as blk:
            @blk.tensor
            def _(h):
                for f in self.ops["te"]:
                    f(h)

            @blk.vector
            def _(h):
                for f in self.ops["ve"]:
                    f(h)

            @blk.scalar
            def _(h):
                for f in self.ops["se"]:
                    f(h)

            @blk.sync
            def _(h):
                for f in self.ops["sy"]:
                    f(h)

            @blk.gpsimd
            def _(h):
                for f in self.ops["gp"]:
                    f(h)
                for s, v in waits:
                    h.wait_ge(s, v)


class Psum:
    def __init__(self, banks):
        self.banks = banks
        self.n = len(banks)
        self.i = 0

    def get(self):
        h, b = self.banks[self.i % self.n]
        self.i += 1
        return h, b


class View:
    __slots__ = ("t", "c0", "w", "b")
    def __init__(self, t, c0, w):
        self.t, self.c0, self.w = t, c0, w
        self.b = Buf()

    def ap(self, a=0, b_=None):
        b_ = self.w if b_ is None else b_
        return self.t.ap()[:, self.c0 + a: self.c0 + b_]


def build_program(Thalf=1024):
    nc = bass.Bass("TRN2", target_bir_lowering=False)
    NCC = Thalf // CH
    RH = Thalf // 2
    NT_R = RH // 128
    SW_R = min(512, RH)
    NSL_R = RH // SW_R
    SW_F = min(256, Thalf)
    NSL_F = Thalf // SW_F
    NTF = SW_F // 128

    dI = {}
    def din(name, shape, dt=F32):
        dI[name] = nc.dram_tensor(name, list(shape), dt, kind="ExternalInput").ap()
        return dI[name]

    x_own = din("x_own", (Thalf, D))
    x_hist = din("x_hist", (Thalf, D))
    wq = din("wq", (NDC, 128, NDC * 128))
    wk = din("wk", (NDC, 128, NDC * 128))
    wv = din("wv", (NDC, 128, NDC * 128))
    wo = din("wo", (D, D))
    wfa = din("wfa", (D, 128))
    wfb = din("wfb", (128, D))
    wb_ = din("wb", (D, 16))
    wga = din("wga", (D, 128))
    wgb = din("wgb", (128, D))
    wgu = din("wgu", (128, 128, NDC * 128))
    wdn = din("wdn", (D * 4, D))
    convq = din("convq", (D, 4))
    convk = din("convk", (D, 4))
    convv = din("convv", (D, 4))
    dtb = din("dtb", (D,))
    alog = din("alog", (128, H))
    normw = din("normw", (128,))
    CNS = _consts()
    cdram = {k: din("c_" + k, v.shape,
                    mybir.dt.uint8 if v.dtype == np.uint8 else F32)
             for k, v in CNS.items()}

    out = nc.dram_tensor("out", [Thalf, D], F32, kind="ExternalOutput").ap()

    sp_k = nc.dram_tensor("sp_k", [2, H, NCC, 128, 128], F32).ap()
    sp_v = nc.dram_tensor("sp_v", [2, H, NCC, 128, 128], F32).ap()
    sp_g = nc.dram_tensor("sp_g", [2, H, NCC, 128, 128], F32).ap()
    sp_q = nc.dram_tensor("sp_q", [H, NCC, 128, 128], F32).ap()
    sp_og = nc.dram_tensor("sp_og", [H, NCC, 128, 128], F32).ap()

    kb = KB(nc)
    banks = [(nc.alloc_psum_tensor(f"pb{i}", [128, 512], F32), Buf())
             for i in range(8)]
    ps_scan = Psum(banks[:4])
    ps_main = Psum(banks[4:])
    spk_b = [[[Buf() for _ in range(NCC)] for _ in range(H)] for _ in range(2)]
    spv_b = [[[Buf() for _ in range(NCC)] for _ in range(H)] for _ in range(2)]
    spg_b = [[[Buf() for _ in range(NCC)] for _ in range(H)] for _ in range(2)]
    spq_b = [[Buf() for _ in range(NCC)] for _ in range(H)]
    spog_b = [[Buf() for _ in range(NCC)] for _ in range(H)]
    sbb = [0]

    def sb(name, cols, dt=F32):
        h = nc.alloc_sbuf_tensor(name, [128, cols], dt)
        sbb[0] += cols * (1 if dt == mybir.dt.uint8 else 4)
        return h

    c_mq8 = sb("sc_mq8", 128); c_mk8 = sb("sc_mk8", 128)
    U8 = mybir.dt.uint8
    c_m16 = sb("sc_m16", 128, U8); c_m32 = sb("sc_m32", 128, U8)
    c_m64 = sb("sc_m64", 128, U8); c_m128 = sb("sc_m128", 128, U8)
    c_tri = sb("sc_tri", 128); c_tru = sb("sc_tru", 128)
    c_id = sb("sc_id", 128); c_onec = sb("sc_onec", 1)
    c_eps5 = sb("sc_eps5", 1); c_eps6 = sb("sc_eps6", 1)
    s_wfa = sb("s_wfa", NDC * 128)
    s_wfb = sb("s_wfb", D)
    s_wga = sb("s_wga", NDC * 128)
    s_wgb = sb("s_wgb", D)
    s_wb = sb("s_wb", NDC * 16)
    s_cq = sb("s_cq", NDC * 4); s_ck = sb("s_ck", NDC * 4); s_cv = sb("s_cv", NDC * 4)
    s_dtb = sb("s_dtb", NDC)
    s_al = sb("s_al", H)
    s_nega = sb("s_nega", H)
    s_nw = sb("s_nw", 1)
    cbuf = Buf()

    A1 = sb("A1", 16384)
    AR_BASE = [("ltm", 128), ("lfm", 128), ("e2t", 128), ("knf", 128),
               ("wneg", 128), ("qt", 128), ("rtm", 128), ("ecp", 128),
               ("ebp", 128), ("xk", 128), ("xq", 128), ("yy", 128),
               ("aq", 128), ("akn", 128), ("emp", 128), ("emm", 128),
               ("q6", 128), ("r1", 128), ("r2", 128), ("r3", 128),
               ("r4", 128), ("r5", 128),
               ("cmid", 16), ("rms", 1), ("pcol", 1), ("bneg", 1)]
    ALIAS = {"sgt": "ltm", "z0": "e2t", "z1": "knf", "otm": "ecp",
             "og": "ebp", "q0": "akn", "q1": "emp", "q2": "emm",
             "q3": "xk", "q4": "xq", "q5": "yy", "r0": "lfm"}
    AR_W = sum(w for _, w in AR_BASE)
    A2 = sb("A2", 2 * AR_W + 6 * SW_F)
    arviews = []
    for s in (0, 1):
        off = s * AR_W
        d = {}
        for nm, w in AR_BASE:
            d[nm] = View(A2, off, w)
            off += w
        for a, bnm in ALIAS.items():
            d[a] = d[bnm]
        arviews.append(d)
    fti = [View(A2, 2 * AR_W + i * SW_F, SW_F) for i in range(4)]
    gup = [View(A2, 2 * AR_W + (4 + i) * SW_F, SW_F) for i in range(2)]

    pscr = sb("pscr", 2 * (RH + 3) + 2 * RH)
    pre = [View(pscr, 0, RH + 3), View(pscr, RH + 3, RH + 3)]
    post2 = [View(pscr, 2 * (RH + 3), RH), View(pscr, 2 * (RH + 3) + RH, RH)]
    lnd = sb("lnd", 1024)
    lnd_b = Buf()
    qstg = sb("qstg", 512)
    qstg_b = [Buf(), Buf(), Buf(), Buf()]
    sqdum = sb("sqdum", 128)
    sqdum_b = Buf()

    mid_f = sb("mid_f", Thalf); mid_b = [Buf(), Buf()]
    mid2_f = sb("mid2_f", Thalf); mid2_b = [Buf(), Buf()]
    beta_t = sb("beta_t", 2 * NCC * 16)
    beta_b = [Buf() for _ in range(2 * NCC)]
    xtm = sb("xtm", 2 * D)
    xtm_b = [Buf(), Buf()]
    wdb = sb("wdb", 3 * 2048)
    wdb_b = [Buf(), Buf(), Buf()]
    h1tail = sb("h1tail", NDC * 3); h1tail_b = Buf()
    ttm = sb("ttm", 8 * 128); ttm_b = [Buf() for _ in range(8)]
    lncol = sb("lncol", 16); lncol_b = [Buf(), Buf()]
    tcol = sb("tcol", 16); tcol_b = [Buf(), Buf()]
    S_st = sb("S_st", H * 128); S_b = [Buf() for _ in range(H)]
    ogl = sb("ogl", 2 * 128); ogl_b = [Buf(), Buf()]

    assert sbb[0] * 1.0 <= 207 * 1024, f"SBUF {sbb[0]/1024:.1f}KB over budget"

    ldk = [View(A1, s * 4096, 1024) for s in (0, 1)]
    ldv = [View(A1, s * 4096 + 1024, 1024) for s in (0, 1)]
    ldg = [View(A1, s * 4096 + 2048, 1024) for s in (0, 1)]
    ldq = [View(A1, s * 4096 + 3072, 1024) for s in (0, 1)]
    h1v = [View(A1, 8192 + c * RH, RH) for c in range(NDC)]
    h2f = [View(A1, 8192 + c * SW_F, SW_F) for c in range(NDC)]
    dacc = [View(A1, 8192 + NDC * SW_F + t * D, D) for t in range(NTF)]

    def ve(fn, reads=(), rbufs=(), wbufs=()):
        return kb.op("ve", fn, reads, rbufs, wbufs)

    def se(fn, reads=(), rbufs=(), wbufs=()):
        return kb.op("se", fn, reads, rbufs, wbufs)

    def te(fn, reads=(), rbufs=(), wbufs=()):
        return kb.op("te", fn, reads, rbufs, wbufs)

    def gp(fn, reads=(), rbufs=(), wbufs=()):
        return kb.op("gp", fn, reads, rbufs, wbufs)

    def self_w(b):
        if b.w is None:
            return []
        return b.w if isinstance(b.w, list) else [b.w]

    def stitch(dst_views, src_bufs):
        evs = []
        for b in src_bufs:
            evs += self_w(b) + list(b.rs)
        for v in dst_views:
            v.b.w = None
            v.b.rs = list(evs)

    cev = []
    for dst, key in [(c_mq8, "mq8"), (c_mk8, "mk8"), (c_m16, "m16r"),
                     (c_m32, "m32r"), (c_m64, "m64r"), (c_m128, "m128r"),
                     (c_tri, "tri"), (c_tru, "tru"),
                     (c_id, "iden"), (c_onec, "onec")]:
        cev.append(kb.dma("sy", dst.ap(), cdram[key]))
    cev.append(kb.dma("sy", s_wfb.ap(), wfb))
    cev.append(kb.dma("sy", s_wgb.ap(), wgb))
    for c in range(NDC):
        rr = lambda t: t.rearrange("(c p) f -> c p f", p=128)[c]
        cev.append(kb.dma("sy", s_wfa.ap()[:, c * 128:(c + 1) * 128], rr(wfa)))
        cev.append(kb.dma("sy", s_wga.ap()[:, c * 128:(c + 1) * 128], rr(wga)))
        cev.append(kb.dma("sy", s_wb.ap()[:, c * 16:(c + 1) * 16], rr(wb_)))
        cev.append(kb.dma("sy", s_cq.ap()[:, c * 4:(c + 1) * 4], rr(convq)))
        cev.append(kb.dma("sy", s_ck.ap()[:, c * 4:(c + 1) * 4], rr(convk)))
        cev.append(kb.dma("sy", s_cv.ap()[:, c * 4:(c + 1) * 4], rr(convv)))
        cev.append(kb.dma("sy", s_dtb.ap()[:, c:c + 1],
                   dtb.rearrange("(c p) -> c p", p=128)[c].unsqueeze(1)))
    al_ev = kb.dma("sy", s_al.ap(), alog)
    cev.append(al_ev)
    cev.append(kb.dma("sy", s_nw.ap(), normw.unsqueeze(1)))
    cev.append(ve(lambda h: h.memset(c_eps5.ap(), 1e-5)))
    cev.append(ve(lambda h: h.memset(c_eps6.ap(), 1e-6)))
    e1 = se(lambda h: h.activation(s_nega.ap(), s_al.ap(), AF.Exp), reads=(al_ev,))
    cev.append(ve(lambda h: h.tensor_scalar(s_nega.ap(), s_nega.ap(), -1.0, None,
                                            ALU.mult), reads=(e1,)))
    cbuf.w = cev
    for hh in range(H):
        ve(lambda h, hh=hh: h.memset(S_st.ap()[:, hh * 128:(hh + 1) * 128], 0.0),
           wbufs=(S_b[hh],))
    ve(lambda h: h.memset(h1tail.ap(), 0.0), wbufs=(h1tail_b,))

    def pair_rr(gens):
        gens = list(gens)
        while gens:
            for g in list(gens):
                try:
                    next(g)
                except StopIteration:
                    gens.remove(g)
            yield

    def run_units(*gws):
        gws = [list(gw) for gw in gws]
        while gws:
            for gw in list(gws):
                g, w = gw
                for _ in range(w):
                    try:
                        next(g)
                    except StopIteration:
                        gws.remove(gw)
                        break

    def produce_round_gen(seg, rnd, xdram):
        t0 = rnd * NT_R
        own = seg == 1

        def ln_gen(tcr):
            tc = t0 + tcr
            par = tc % 2
            co = par * 8
            lcb = lncol_b[par]
            xb = xtm_b[par]
            xap = xtm.ap()[:, par * D:(par + 1) * D]
            kb.dma("sy", xap, xdram.rearrange("(a p) d -> a p d", p=128)[tc],
                   wbufs=(xb,))
            ve(lambda h: h.tensor_scalar(
                xap[:, 0:D:2], xap[:, 0:D:2], 1.0, 0.0, ALU.mult, ALU.add,
                accum_out=lncol.ap()[:, co:co + 1]),
               wbufs=(lcb, xb))
            ve(lambda h: h.tensor_scalar(
                xap[:, 1:D:2], xap[:, 1:D:2], 1.0, 0.0, ALU.mult, ALU.add,
                accum_out=lncol.ap()[:, co + 4:co + 5]),
               reads=(lcb.w,), wbufs=(lcb, xb))
            se(lambda h: h.activation(
                lnd.ap(), xap[:, 0:D:2], AF.Square,
                accum_out=lncol.ap()[:, co + 1:co + 2]),
               reads=(lcb.w,), rbufs=(xb,), wbufs=(lcb, lnd_b))
            se(lambda h: h.activation(
                lnd.ap(), xap[:, 1:D:2], AF.Square,
                accum_out=lncol.ap()[:, co + 5:co + 6]),
               reads=(lcb.w,), rbufs=(xb,), wbufs=(lcb, lnd_b))
            ve(lambda h: h.tensor_tensor(lncol.ap()[:, co:co + 2],
                                         lncol.ap()[:, co:co + 2],
                                         lncol.ap()[:, co + 4:co + 6], ALU.add),
               rbufs=(lcb,), wbufs=(lcb,))
            ve(lambda h: h.tensor_scalar(lncol.ap()[:, co + 2:co + 4],
                                         lncol.ap()[:, co:co + 2],
                                         1.0 / D, None, ALU.mult),
               rbufs=(lcb,), wbufs=(lcb,))
            ve(lambda h: h.tensor_tensor(lncol.ap()[:, co + 4:co + 5],
                                         lncol.ap()[:, co + 2:co + 3],
                                         lncol.ap()[:, co + 2:co + 3], ALU.mult),
               rbufs=(lcb,), wbufs=(lcb,))
            ve(lambda h: h.tensor_tensor(lncol.ap()[:, co + 5:co + 6],
                                         lncol.ap()[:, co + 3:co + 4],
                                         lncol.ap()[:, co + 4:co + 5], ALU.subtract),
               rbufs=(lcb,), wbufs=(lcb,))
            se(lambda h: h.activation(lncol.ap()[:, co + 6:co + 7],
                                      lncol.ap()[:, co + 5:co + 6],
                                      AF.Ln, bias=c_eps5.ap()[:, 0:1]),
               rbufs=(lcb, cbuf), wbufs=(lcb,))
            se(lambda h: h.activation(lncol.ap()[:, co + 7:co + 8],
                                      lncol.ap()[:, co + 6:co + 7],
                                      AF.Exp, scale=-0.5),
               rbufs=(lcb,), wbufs=(lcb,))
            ve(lambda h: h.tensor_scalar(
                xap, xap, lncol.ap()[:, co + 2:co + 3], lncol.ap()[:, co + 7:co + 8],
                ALU.subtract, ALU.mult),
               reads=(xb.w,), rbufs=(lcb,), wbufs=(xb,)),
            yield
            for dc in range(NDC):
                pb, pbuf = ps_main.get()
                te(lambda h, pb=pb, dc=dc: h.transpose(
                    pb.ap()[:, 0:128], xap[:, dc * 128:(dc + 1) * 128], c_id.ap()),
                   rbufs=(xb, cbuf), wbufs=(pbuf,))
                dstv = h1v[dc]
                dst = dstv.ap(tcr * 128, (tcr + 1) * 128)
                se(lambda h, pb=pb, dst=dst: h.activation(dst, pb.ap()[:, 0:128],
                                                          AF.Copy),
                   rbufs=(pbuf,), wbufs=(dstv.b,))
                if dc == NDC // 2 - 1:
                    yield

        for a in range(0, NT_R, 2):
            gens = [ln_gen(a)]
            if a + 1 < NT_R:
                gens.append(ln_gen(a + 1))
            yield from pair_rr(gens)

        for sl in range(NSL_R):
            pb, pbuf = ps_main.get()
            for dc in range(NDC):
                te(lambda h, pb=pb, dc=dc, sl=sl: h.matmul(
                    pb.ap()[:, 0:SW_R], s_wfa.ap()[:, dc * 128:(dc + 1) * 128],
                    h1v[dc].ap(sl * SW_R, (sl + 1) * SW_R),
                    start=(dc == 0), stop=(dc == NDC - 1)),
                   reads=(pbuf.w,), rbufs=(h1v[dc].b, cbuf),
                   wbufs=(pbuf,) if dc in (0, NDC - 1) else ())
            a0 = rnd * RH + sl * SW_R
            se(lambda h, pb=pb, a0=a0: h.activation(
                mid_f.ap()[:, a0:a0 + SW_R], pb.ap()[:, 0:SW_R], AF.Copy),
               rbufs=(pbuf,), wbufs=(mid_b[rnd],))
            yield
            if own:
                pb2, pbuf2 = ps_main.get()
                for dc in range(NDC):
                    te(lambda h, pb2=pb2, dc=dc, sl=sl: h.matmul(
                        pb2.ap()[:, 0:SW_R], s_wga.ap()[:, dc * 128:(dc + 1) * 128],
                        h1v[dc].ap(sl * SW_R, (sl + 1) * SW_R),
                        start=(dc == 0), stop=(dc == NDC - 1)),
                       reads=(pbuf2.w,), rbufs=(h1v[dc].b, cbuf),
                       wbufs=(pbuf2,) if dc in (0, NDC - 1) else ())
                se(lambda h, pb2=pb2, a0=a0: h.activation(
                    mid2_f.ap()[:, a0:a0 + SW_R], pb2.ap()[:, 0:SW_R], AF.Copy),
                   rbufs=(pbuf2,), wbufs=(mid2_b[rnd],))
                yield

        for tcr in range(NT_R):
            pb, pbuf = ps_main.get()
            for dc in range(NDC):
                te(lambda h, pb=pb, dc=dc, tcr=tcr: h.matmul(
                    pb.ap()[:, 0:16], h1v[dc].ap(tcr * 128, (tcr + 1) * 128),
                    s_wb.ap()[:, dc * 16:(dc + 1) * 16],
                    start=(dc == 0), stop=(dc == NDC - 1)),
                   reads=(pbuf.w,), rbufs=(h1v[dc].b, cbuf),
                   wbufs=(pbuf,) if dc in (0, NDC - 1) else ())
            bidx = seg * NCC + t0 + tcr
            se(lambda h, pb=pb, bidx=bidx: h.activation(
                beta_t.ap()[:, bidx * 16:(bidx + 1) * 16], pb.ap()[:, 0:16],
                AF.Exp, scale=-1.0),
               rbufs=(pbuf,), wbufs=(beta_b[bidx],))
            ve(lambda h, bidx=bidx: h.tensor_scalar(
                beta_t.ap()[:, bidx * 16:(bidx + 1) * 16],
                beta_t.ap()[:, bidx * 16:(bidx + 1) * 16], 1.0, None, ALU.add),
               reads=(beta_b[bidx].w,), wbufs=(beta_b[bidx],))
            ve(lambda h, bidx=bidx: h.reciprocal(
                beta_t.ap()[:, bidx * 16:(bidx + 1) * 16],
                beta_t.ap()[:, bidx * 16:(bidx + 1) * 16]),
               reads=(beta_b[bidx].w,), wbufs=(beta_b[bidx],))
            if tcr % 2 == 1 or tcr == NT_R - 1:
                yield

        def glog_gen(j):
            par = j % 2
            pon = post2[par]
            for sl in range(NSL_R):
                pb, pbuf = ps_main.get()
                te(lambda h, pb=pb, j=j, sl=sl: h.matmul(
                    pb.ap()[:, 0:SW_R], s_wfb.ap()[:, j * 128:(j + 1) * 128],
                    mid_f.ap()[:, rnd * RH + sl * SW_R:rnd * RH + (sl + 1) * SW_R],
                    start=True, stop=True),
                   rbufs=(mid_b[rnd], cbuf), wbufs=(pbuf,))
                dst = pon.ap(sl * SW_R, (sl + 1) * SW_R)
                se(lambda h, pb=pb, dst=dst, j=j: h.activation(
                    dst, pb.ap()[:, 0:SW_R], AF.Exp, bias=s_dtb.ap()[:, j:j + 1]),
                   rbufs=(pbuf, cbuf), wbufs=(pon.b,))
                se(lambda h, dst=dst: h.activation(dst, dst, AF.Ln,
                                                   bias=c_onec.ap()[:, 0:1]),
                   reads=(pon.b.w,), rbufs=(cbuf,), wbufs=(pon.b,))
                ve(lambda h, dst=dst, j=j: h.tensor_scalar(
                    dst, dst, s_nega.ap()[:, j:j + 1], None, ALU.mult),
                   reads=(pon.b.w,), rbufs=(cbuf,), wbufs=(pon.b,))
            yield
            for tcr in range(NT_R):
                pb, pbuf = ps_main.get()
                te(lambda h, pb=pb, pon=pon, tcr=tcr: h.transpose(
                    pb.ap()[:, 0:128], pon.ap(tcr * 128, (tcr + 1) * 128),
                    c_id.ap()),
                   rbufs=(pon.b, cbuf), wbufs=(pbuf,))
                tslot = par * 4 + tcr
                tb = ttm_b[tslot]
                tap = ttm.ap()[:, tslot * 128:(tslot + 1) * 128]
                se(lambda h, pb=pb, tap=tap: h.activation(tap, pb.ap()[:, 0:128],
                                                          AF.Copy),
                   rbufs=(pbuf,), wbufs=(tb,))
                kb.dma("gp", sp_g[seg, j, t0 + tcr], tap, rbufs=(tb,),
                       wbufs=(spg_b[seg][j][t0 + tcr],))
                if tcr % 2 == 1 or tcr == NT_R - 1:
                    yield
        for a in range(0, NDC, 2):
            yield from pair_rr([glog_gen(a), glog_gen(a + 1)])

        def proj_gen(wdram, conv_sb, spill, spbufs, do_norm, extra_scale, is_q, j):
            par = j % 2
            wb2 = wdb_b[par]
            wap = wdb.ap()[:, par * 2048:(par + 1) * 2048]
            kb.dma("sy", wap, wdram[j], wbufs=(wb2,))
            pv = pre[par]
            pon = post2[par]
            for sl in range(NSL_R):
                pb, pbuf = ps_main.get()
                for dc in range(NDC):
                    te(lambda h, pb=pb, dc=dc, sl=sl, wap=wap: h.matmul(
                        pb.ap()[:, 0:SW_R], wap[:, dc * 128:(dc + 1) * 128],
                        h1v[dc].ap(sl * SW_R, (sl + 1) * SW_R),
                        start=(dc == 0), stop=(dc == NDC - 1)),
                       reads=(pbuf.w,), rbufs=(h1v[dc].b, wb2),
                       wbufs=(pbuf,) if dc in (0, NDC - 1) else ())
                se(lambda h, pb=pb, pv=pv, sl=sl: h.activation(
                    pv.ap(3 + sl * SW_R, 3 + (sl + 1) * SW_R), pb.ap()[:, 0:SW_R],
                    AF.Copy),
                   rbufs=(pbuf,), wbufs=(pv.b,))
            pb, pbuf = ps_main.get()
            for dc in range(NDC):
                te(lambda h, pb=pb, dc=dc, wap=wap: h.matmul(
                    pb.ap()[:, 0:3], wap[:, dc * 128:(dc + 1) * 128],
                    h1tail.ap()[:, dc * 3:(dc + 1) * 3],
                    start=(dc == 0), stop=(dc == NDC - 1)),
                   reads=(pbuf.w,), rbufs=(h1tail_b, wb2),
                   wbufs=(pbuf,) if dc in (0, NDC - 1) else ())
            se(lambda h, pb=pb, pv=pv: h.activation(
                pv.ap(0, 3), pb.ap()[:, 0:3], AF.Copy),
               rbufs=(pbuf,), wbufs=(pv.b,))
            yield
            pp = pon.ap(0, RH)
            ve(lambda h, pv=pv, pp=pp, j=j: h.tensor_scalar(
                pp, pv.ap(0, RH), conv_sb.ap()[:, j * 4:j * 4 + 1],
                None, ALU.mult),
               rbufs=(pv.b, cbuf), wbufs=(pon.b,))
            for d in (1, 2):
                ve(lambda h, pv=pv, pp=pp, j=j, d=d: h.scalar_tensor_tensor(
                    pp, pv.ap(d, d + RH), conv_sb.ap()[:, j * 4 + d:j * 4 + d + 1],
                    pp, ALU.mult, ALU.add),
                   reads=(pon.b.w,), rbufs=(pv.b, cbuf), wbufs=(pon.b,))
            ve(lambda h, pv=pv, pp=pp, j=j: h.scalar_tensor_tensor(
                pp, pv.ap(3, 3 + RH), conv_sb.ap()[:, j * 4 + 3:j * 4 + 4],
                pp, ALU.mult, ALU.add),
               reads=(pon.b.w,), rbufs=(pv.b, cbuf), wbufs=(pon.b,))
            se(lambda h, pp=pp: h.activation(pp, pp, AF.Silu),
               reads=(pon.b.w,), wbufs=(pon.b,))
            yield
            g0 = par * 8
            tcb = tcol_b[par]
            taps = []
            for tcr in range(NT_R):
                pb, pbuf = ps_main.get()
                te(lambda h, pb=pb, pon=pon, tcr=tcr: h.transpose(
                    pb.ap()[:, 0:128], pon.ap(tcr * 128, (tcr + 1) * 128),
                    c_id.ap()),
                   rbufs=(pon.b, cbuf), wbufs=(pbuf,))
                tslot = par * 4 + tcr
                tap = ttm.ap()[:, tslot * 128:(tslot + 1) * 128]
                tb = ttm_b[tslot]
                taps.append((tap, tb))
                if do_norm:
                    se(lambda h, pb=pb, tcr=tcr: h.activation(
                        sqdum.ap(), pb.ap()[:, 0:128],
                        AF.Square, accum_out=tcol.ap()[:, g0 + tcr:g0 + tcr + 1]),
                       reads=(tcb.w,), rbufs=(pbuf,), wbufs=(sqdum_b, tcb))
                se(lambda h, pb=pb, tap=tap: h.activation(
                    tap, pb.ap()[:, 0:128], AF.Copy),
                   rbufs=(pbuf,), wbufs=(tb,))
                if tcr % 2 == 1:
                    yield
            if do_norm:
                se(lambda h: h.activation(tcol.ap()[:, g0 + 4:g0 + 4 + NT_R],
                                          tcol.ap()[:, g0:g0 + NT_R],
                                          AF.Ln, bias=c_eps6.ap()[:, 0:1]),
                   rbufs=(tcb, cbuf), wbufs=(tcb,))
                se(lambda h: h.activation(tcol.ap()[:, g0 + 4:g0 + 4 + NT_R],
                                          tcol.ap()[:, g0 + 4:g0 + 4 + NT_R],
                                          AF.Exp, scale=-0.5),
                   reads=(tcb.w,), wbufs=(tcb,))
                if extra_scale != 1.0:
                    ve(lambda h: h.tensor_scalar(
                        tcol.ap()[:, g0 + 4:g0 + 4 + NT_R],
                        tcol.ap()[:, g0 + 4:g0 + 4 + NT_R],
                        float(extra_scale), None, ALU.mult),
                       reads=(tcb.w,), wbufs=(tcb,))
            for tcr in range(NT_R):
                tap, tb = taps[tcr]
                if do_norm:
                    ve(lambda h, tap=tap, tcr=tcr: h.tensor_scalar(
                        tap, tap, tcol.ap()[:, g0 + 4 + tcr:g0 + 5 + tcr],
                        None, ALU.mult),
                       reads=(tb.w,), rbufs=(tcb,), wbufs=(tb,))
                if is_q:
                    pb2, pbuf2 = ps_main.get()
                    te(lambda h, pb2=pb2, tap=tap: h.transpose(
                        pb2.ap()[:, 0:128], tap, c_id.ap()),
                       rbufs=(tb, cbuf), wbufs=(pbuf2,))
                    qsl = par * 2 + tcr % 2
                    fap = qstg.ap()[:, qsl * 128:(qsl + 1) * 128]
                    qb = qstg_b[qsl]
                    se(lambda h, pb2=pb2, fap=fap: h.activation(
                        fap, pb2.ap()[:, 0:128], AF.Copy),
                       rbufs=(pbuf2,), wbufs=(qb,))
                    kb.dma("gp", spill[j, t0 + tcr], fap, rbufs=(qb,),
                           wbufs=(spq_b[j][t0 + tcr],))
                else:
                    kb.dma("gp", spill[seg, j, t0 + tcr], tap, rbufs=(tb,),
                           wbufs=(spbufs[seg][j][t0 + tcr],))
                if tcr % 2 == 1 or tcr == NT_R - 1:
                    yield

        for wdram, conv_sb, spill, spbufs, do_norm, esc, is_q in (
                (wk, s_ck, sp_k, spk_b, True, 1.0, False),
                (wv, s_cv, sp_v, spv_b, False, 1.0, False),
                (wq, s_cq, sp_q, None, True, float(DK) ** -0.5, True)):
            if is_q and not own:
                continue
            for a in range(0, NDC, 2):
                yield from pair_rr(
                    [proj_gen(wdram, conv_sb, spill, spbufs, do_norm, esc, is_q, a),
                     proj_gen(wdram, conv_sb, spill, spbufs, do_norm, esc, is_q,
                              a + 1)])

        for dc in range(NDC):
            ve(lambda h, dc=dc: h.tensor_scalar(
                h1tail.ap()[:, dc * 3:(dc + 1) * 3], h1v[dc].ap(RH - 3, RH),
                1.0, None, ALU.mult),
               rbufs=(h1v[dc].b,), wbufs=(h1tail_b,))
        yield

    TWO_M24 = float(2.0 ** -24)

    def scan_chunk_gen(seg, cc, hh, slot):
        own = (seg == 1)
        A = arviews[slot]
        def ai(nm):
            return A[nm].ap()
        def ab(nm):
            return A[nm].b
        hb = hh // 8
        lh = hh % 8
        kt = ldk[hb].ap(lh * 128, (lh + 1) * 128)
        vt = ldv[hb].ap(lh * 128, (lh + 1) * 128)
        gt = ldg[hb].ap(lh * 128, (lh + 1) * 128)
        qt_l = ldq[hb].ap(lh * 128, (lh + 1) * 128)
        ktb, vtb, gtb, qtb = ldk[hb].b, ldv[hb].b, ldg[hb].b, ldq[hb].b
        bidx = seg * NCC + cc
        bcol = beta_t.ap()[:, bidx * 16 + hh: bidx * 16 + hh + 1]
        rnd = cc // NT_R

        pbL, pbLb = ps_scan.get()
        te(lambda h: h.matmul(pbL.ap()[:, 0:128], c_tri.ap(), gt,
                              start=True, stop=True),
           rbufs=(cbuf, gtb), wbufs=(pbLb,))
        te(lambda h: h.matmul(pbL.ap()[:, 128:256], c_tru.ap(), gt,
                              start=True, stop=True),
           rbufs=(cbuf, gtb), wbufs=(pbLb,))
        te(lambda h: h.matmul(pbL.ap()[:, 256:257], gt, c_onec.ap(),
                              start=True, stop=True),
           rbufs=(cbuf, gtb), wbufs=(pbLb,))
        se(lambda h: h.activation(ai("ltm"), pbL.ap()[:, 0:128], AF.Copy),
           rbufs=(pbLb,), wbufs=(ab("ltm"),))
        se(lambda h: h.activation(ai("e2t"), pbL.ap()[:, 128:256], AF.Exp),
           rbufs=(pbLb,), wbufs=(ab("e2t"),))
        se(lambda h: h.activation(ai("pcol"), pbL.ap()[:, 256:257], AF.Exp),
           rbufs=(pbLb,), wbufs=(ab("pcol"),))
        pbF, pbFb = ps_scan.get()
        te(lambda h: h.transpose(pbF.ap()[:, 0:128], ai("ltm"), c_id.ap()),
           rbufs=(ab("ltm"), cbuf), wbufs=(pbFb,))
        te(lambda h: h.transpose(pbF.ap()[:, 128:256], kt, c_id.ap()),
           rbufs=(ktb, cbuf), wbufs=(pbFb,))
        se(lambda h: h.activation(ai("lfm"), pbF.ap()[:, 0:128], AF.Copy),
           rbufs=(pbFb,), wbufs=(ab("lfm"),))
        se(lambda h: h.activation(ai("knf"), pbF.ap()[:, 128:256], AF.Copy),
           rbufs=(pbFb,), wbufs=(ab("knf"),))
        lfm = ai("lfm")
        se(lambda h: h.activation(ai("ecp"), lfm, AF.Exp),
           rbufs=(ab("lfm"),), wbufs=(ab("ecp"),))
        ve(lambda h: h.scalar_tensor_tensor(ai("wneg"), ai("ecp"), -1.0, ai("knf"),
                                            ALU.mult, ALU.mult),
           rbufs=(ab("ecp"), ab("knf")), wbufs=(ab("wneg"),))
        if own:
            ve(lambda h: h.tensor_tensor(ai("qt"), qt_l, ai("ecp"), ALU.mult),
               rbufs=(qtb, ab("ecp")), wbufs=(ab("qt"),))
        ve(lambda h: h.tensor_tensor(ai("rtm"), kt, ai("e2t"), ALU.mult),
           rbufs=(ktb, ab("e2t")), wbufs=(ab("rtm"),))
        ve(lambda h: h.tensor_scalar(ai("bneg"), bcol, -1.0, None, ALU.mult),
           rbufs=(beta_b[bidx],), wbufs=(ab("bneg"),))

        yield
        cm = ai("cmid")
        ve(lambda h: h.tensor_scalar(cm, lfm[:, 7:128:8], 0.5, None, ALU.mult),
           rbufs=(ab("lfm"),), wbufs=(ab("cmid"),))
        ve(lambda h: h.scalar_tensor_tensor(cm[:, 1:16], lfm[:, 7:120:8], 0.5,
                                            cm[:, 1:16], ALU.mult, ALU.add),
           reads=(ab("cmid").w,), rbufs=(ab("lfm"),), wbufs=(ab("cmid"),))
        cmb = cm.unsqueeze(2).broadcast_to([128, 16, 8])
        lfm3 = lfm.rearrange("p (a b) -> p a b", b=8)
        ve(lambda h: h.tensor_tensor(ai("emp").rearrange("p (a b) -> p a b", b=8),
                                     lfm3, cmb, ALU.subtract),
           rbufs=(ab("lfm"), ab("cmid")), wbufs=(ab("emp"),))
        se(lambda h: h.activation(ai("emp"), ai("emp"), AF.Exp),
           reads=(ab("emp").w,), wbufs=(ab("emp"),))
        ve(lambda h: h.tensor_tensor(ai("emm").rearrange("p (a b) -> p a b", b=8),
                                     cmb, lfm3, ALU.subtract),
           rbufs=(ab("lfm"), ab("cmid")), wbufs=(ab("emm"),))
        se(lambda h: h.activation(ai("emm"), ai("emm"), AF.Exp),
           reads=(ab("emm").w,), wbufs=(ab("emm"),))
        ve(lambda h: h.scalar_tensor_tensor(ai("xk"), ai("emp"), TWO_M24, ai("knf"),
                                            ALU.mult, ALU.mult),
           rbufs=(ab("emp"), ab("knf")), wbufs=(ab("xk"),))
        ve(lambda h: h.scalar_tensor_tensor(ai("yy"), ai("emm"), TWO_M24, ai("knf"),
                                            ALU.mult, ALU.mult),
           rbufs=(ab("emm"), ab("knf")), wbufs=(ab("yy"),))
        pbMK, pbMKb = ps_scan.get()
        te(lambda h: h.matmul(pbMK.ap()[:, 0:128], ai("yy"), ai("xk"),
                              start=True, stop=True),
           rbufs=(ab("yy"), ab("xk")), wbufs=(pbMKb,))
        if own:
            ve(lambda h: h.scalar_tensor_tensor(ai("xq"), ai("emp"), TWO_M24, qt_l,
                                                ALU.mult, ALU.mult),
               rbufs=(ab("emp"), qtb), wbufs=(ab("xq"),))
            te(lambda h: h.matmul(pbMK.ap()[:, 128:256], ai("yy"), ai("xq"),
                                  start=True, stop=True),
               rbufs=(ab("yy"), ab("xq")), wbufs=(pbMKb,))
        ve(lambda h: h.tensor_tensor(ai("akn"), pbMK.ap()[:, 0:128], c_mk8.ap(),
                                     ALU.mult),
           rbufs=(pbMKb, cbuf), wbufs=(ab("akn"),))
        if own:
            ve(lambda h: h.tensor_tensor(ai("aq"), pbMK.ap()[:, 128:256], c_mq8.ap(),
                                         ALU.mult),
               rbufs=(pbMKb, cbuf), wbufs=(ab("aq"),))

        def build_cp(l):
            nb = 128 // l
            ve(lambda h, l=l: h.tensor_scalar(ai("ecp")[:, 0:l], lfm[:, 0:l],
                                              1.0, None, ALU.mult),
               rbufs=(ab("lfm"),), wbufs=(ab("ecp"),))
            ve(lambda h, l=l, nb=nb: h.tensor_tensor(
                ai("ecp")[:, l:128].rearrange("p (a b) -> p a b", b=l),
                lfm[:, l:128].rearrange("p (a b) -> p a b", b=l),
                lfm[:, l - 1:128 - l:l].unsqueeze(2).broadcast_to([128, nb - 1, l]),
                ALU.subtract),
               reads=(ab("ecp").w,), rbufs=(ab("lfm"),), wbufs=(ab("ecp"),))
            se(lambda h: h.activation(ai("ecp"), ai("ecp"), AF.Exp),
               reads=(ab("ecp").w,), wbufs=(ab("ecp"),))
            ve(lambda h, l=l, nb=nb: h.tensor_tensor(
                ai("ebp").rearrange("p (a b) -> p a b", b=l),
                lfm[:, l - 1:128:l].unsqueeze(2).broadcast_to([128, nb, l]),
                lfm.rearrange("p (a b) -> p a b", b=l),
                ALU.subtract),
               rbufs=(ab("lfm"),), wbufs=(ab("ebp"),))
            se(lambda h: h.activation(ai("ebp"), ai("ebp"), AF.Exp),
               reads=(ab("ebp").w,), wbufs=(ab("ebp"),))
            gp(lambda h: h.tensor_tensor(ai("xk"), ai("knf"), ai("ecp"), ALU.mult),
               rbufs=(ab("knf"), ab("ecp")), wbufs=(ab("xk"),))
            ve(lambda h: h.tensor_tensor(ai("yy"), ai("knf"), ai("ebp"), ALU.mult),
               rbufs=(ab("knf"), ab("ebp")), wbufs=(ab("yy"),))
            if own:
                gp(lambda h: h.tensor_tensor(ai("xq"), qt_l, ai("ecp"), ALU.mult),
                   rbufs=(qtb, ab("ecp")), wbufs=(ab("xq"),))

        yield
        for l, mask_c in ((8, c_m16), (16, c_m32), (32, c_m64), (64, c_m128)):
            yield
            build_cp(l)
            pbK2, pbK2b = ps_scan.get()
            te(lambda h, pbK2=pbK2: h.matmul(pbK2.ap()[:, 0:128], ai("yy"), ai("xk"),
                                             start=True, stop=True),
               rbufs=(ab("yy"), ab("xk")), wbufs=(pbK2b,))
            if own:
                te(lambda h, pbK2=pbK2: h.matmul(pbK2.ap()[:, 128:256], ai("yy"),
                                                 ai("xq"), start=True, stop=True),
                   rbufs=(ab("yy"), ab("xq")), wbufs=(pbK2b,))
            ve(lambda h, pbK2=pbK2, mask_c=mask_c: h.copy_predicated(
                ai("akn"), mask_c.ap(), pbK2.ap()[:, 0:128]),
               reads=(ab("akn").w,), rbufs=(pbK2b, cbuf), wbufs=(ab("akn"),))
            if own:
                ve(lambda h, pbK2=pbK2, mask_c=mask_c: h.copy_predicated(
                    ai("aq"), mask_c.ap(), pbK2.ap()[:, 128:256]),
                   reads=(ab("aq").w,), rbufs=(pbK2b, cbuf), wbufs=(ab("aq"),))

        ve(lambda h: h.tensor_scalar(ai("akn"), ai("akn"), ai("bneg"), None,
                                     ALU.mult),
           reads=(ab("akn").w,), rbufs=(ab("bneg"),), wbufs=(ab("akn"),))

        yield
        qn = [f"q{i}" for i in range(7)]
        rn = [f"r{i}" for i in range(6)]
        pbT, pbTb = ps_scan.get()
        te(lambda h: h.transpose(pbT.ap()[:, 0:128], ai("q0"), c_id.ap()),
           rbufs=(ab("q0"), cbuf), wbufs=(pbTb,))
        se(lambda h: h.activation(ai("r0"), pbT.ap()[:, 0:128], AF.Copy),
           rbufs=(pbTb,), wbufs=(ab("r0"),))
        for i in range(6):
            yield
            pbA, pbAb = ps_scan.get()
            te(lambda h, i=i, pbA=pbA: h.matmul(pbA.ap()[:, 0:128], ai(rn[i]),
                                                ai(qn[i]), start=True, stop=True),
               rbufs=(ab(rn[i]), ab(qn[i])), wbufs=(pbAb,))
            if i < 5:
                te(lambda h, i=i, pbA=pbA: h.matmul(pbA.ap()[:, 128:256], ai(qn[i]),
                                                    ai(rn[i]), start=True, stop=True),
                   rbufs=(ab(qn[i]), ab(rn[i])), wbufs=(pbAb,))
            se(lambda h, i=i, pbA=pbA: h.activation(ai(qn[i + 1]),
                                                    pbA.ap()[:, 0:128], AF.Copy),
               rbufs=(pbAb,), wbufs=(ab(qn[i + 1]),))
            if i < 5:
                se(lambda h, i=i, pbA=pbA: h.activation(ai(rn[i + 1]),
                                                        pbA.ap()[:, 128:256], AF.Copy),
                   rbufs=(pbAb,), wbufs=(ab(rn[i + 1]),))

        yield
        Sap = S_st.ap()[:, hh * 128:(hh + 1) * 128]
        pbW, pbWb = ps_scan.get()
        te(lambda h: h.matmul(pbW.ap()[:, 0:128], ai("wneg"), Sap,
                              start=True, stop=True),
           rbufs=(ab("wneg"), S_b[hh]), wbufs=(pbWb,))
        ve(lambda h: h.tensor_add(ai("z0"), vt, pbW.ap()[:, 0:128]),
           rbufs=(vtb, pbWb), wbufs=(ab("z0"),))
        zs = ["z0", "z1"]
        cur = 0
        for i in range(6, -1, -1):
            yield
            pbZ, pbZb = ps_scan.get()
            te(lambda h, i=i, pbZ=pbZ, cur=cur: h.matmul(
                pbZ.ap()[:, 0:128], ai(qn[i]), ai(zs[cur]), start=True, stop=True),
               rbufs=(ab(qn[i]), ab(zs[cur])), wbufs=(pbZb,))
            ve(lambda h, pbZ=pbZ, cur=cur: h.tensor_add(
                ai(zs[1 - cur]), ai(zs[cur]), pbZ.ap()[:, 0:128]),
               rbufs=(ab(zs[cur]), pbZb), wbufs=(ab(zs[1 - cur]),))
            cur = 1 - cur
        u_nm = zs[1 - cur]
        ve(lambda h: h.tensor_scalar(ai(u_nm), ai(zs[cur]), bcol, None,
                                     ALU.mult),
           rbufs=(ab(zs[cur]), beta_b[bidx]), wbufs=(ab(u_nm),))

        yield
        if own:
            pbO, pbOb = ps_scan.get()
            te(lambda h: h.matmul(pbO.ap()[:, 0:128], ai("qt"), Sap,
                                  start=True, stop=False),
               rbufs=(ab("qt"), S_b[hh]), wbufs=(pbOb,))
            e_o = te(lambda h: h.matmul(pbO.ap()[:, 0:128], ai("aq"), ai(u_nm),
                                        start=False, stop=True),
                     reads=(pbOb.w,), rbufs=(ab("aq"), ab(u_nm)))
            pbOb.w = e_o
            se(lambda h: h.activation(ai("otm"), pbO.ap()[:, 0:128], AF.Copy),
               rbufs=(pbOb,), wbufs=(ab("otm"),))
            se(lambda h: h.activation(ai("og"), pbO.ap()[:, 0:128], AF.Square,
                                      accum_out=cm[:, 0:1]),
               reads=(ab("cmid").w,), rbufs=(pbOb,), wbufs=(ab("og"), ab("cmid")))
            se(lambda h: h.activation(cm[:, 1:2], cm[:, 0:1],
                                      AF.Ln, bias=c_eps6.ap()[:, 0:1],
                                      scale=1.0 / 128.0),
               rbufs=(ab("cmid"), cbuf), wbufs=(ab("cmid"),))
            se(lambda h: h.activation(ai("rms"), cm[:, 1:2], AF.Exp, scale=-0.5),
               rbufs=(ab("cmid"),), wbufs=(ab("rms"),))
            pbG, pbGb = ps_scan.get()
            te(lambda h: h.matmul(pbG.ap()[:, 0:128],
                                  mid2_f.ap()[:, cc * 128:(cc + 1) * 128],
                                  s_wgb.ap()[:, hh * 128:(hh + 1) * 128],
                                  start=True, stop=True),
               rbufs=(mid2_b[rnd], cbuf), wbufs=(pbGb,))
            se(lambda h: h.activation(ai("sgt"), pbG.ap()[:, 0:128], AF.Exp,
                                      scale=-1.0),
               rbufs=(pbGb,), wbufs=(ab("sgt"),))
            ve(lambda h: h.tensor_scalar(ai("sgt"), ai("sgt"), 1.0, None, ALU.add),
               reads=(ab("sgt").w,), wbufs=(ab("sgt"),))
            ve(lambda h: h.reciprocal(ai("sgt"), ai("sgt")),
               reads=(ab("sgt").w,), wbufs=(ab("sgt"),))
            ve(lambda h: h.tensor_scalar(ai("og"), ai("otm"), ai("rms"), None,
                                         ALU.mult),
               rbufs=(ab("otm"), ab("rms")), wbufs=(ab("og"),))
            ve(lambda h: h.tensor_tensor(ai("og"), ai("og"), ai("sgt"), ALU.mult),
               reads=(ab("og").w,), rbufs=(ab("sgt"),), wbufs=(ab("og"),))
            pbH, pbHb = ps_scan.get()
            te(lambda h: h.transpose(pbH.ap()[:, 0:128], ai("og"), c_id.ap()),
               rbufs=(ab("og"), cbuf), wbufs=(pbHb,))
            se(lambda h: h.activation(ai("otm"), pbH.ap()[:, 0:128], AF.Copy,
                                      scale=s_nw.ap()[:, 0:1]),
               rbufs=(pbHb, cbuf), wbufs=(ab("otm"),))
            kb.dma("gp", sp_og[hh, cc], ai("otm"), rbufs=(ab("otm"),),
                   wbufs=(spog_b[hh][cc],))

        pbS, pbSb = ps_scan.get()
        te(lambda h: h.matmul(pbS.ap()[:, 0:128], ai("rtm"), ai(u_nm),
                              start=True, stop=True),
           rbufs=(ab("rtm"), ab(u_nm)), wbufs=(pbSb,))
        ve(lambda h: h.scalar_tensor_tensor(Sap, Sap, ai("pcol"),
                                            pbS.ap()[:, 0:128], ALU.mult, ALU.add),
           reads=(S_b[hh].w,) + tuple(S_b[hh].rs), rbufs=(pbSb, ab("pcol")),
           wbufs=(S_b[hh],))

    def scan_chunks_gen(seg, c_lo, c_hi, done=None):
        for cc in range(c_lo, c_hi):
            for hb in (0, 1):
                for tens, spl, sbv in ((ldk, sp_k, spk_b), (ldv, sp_v, spv_b),
                                       (ldg, sp_g, spg_b)):
                    vw = tens[hb]
                    evs = []
                    for i in range(8):
                        hh = hb * 8 + i
                        ev = kb.dma("sy", vw.ap(i * 128, (i + 1) * 128),
                                    spl[seg, hh, cc],
                                    rbufs=(sbv[seg][hh][cc],),
                                    reads=((tuple(self_w(vw.b)) + tuple(vw.b.rs))
                                           if i == 0 else ()))
                        evs.append(ev)
                    vw.b.w = list(evs)
                    vw.b.rs = []
                if seg == 1:
                    vw = ldq[hb]
                    evs = []
                    for i in range(8):
                        hh = hb * 8 + i
                        ev = kb.dma("sy", vw.ap(i * 128, (i + 1) * 128),
                                    sp_q[hh, cc],
                                    rbufs=(spq_b[hh][cc],),
                                    reads=((tuple(self_w(vw.b)) + tuple(vw.b.rs))
                                           if i == 0 else ()))
                        evs.append(ev)
                    vw.b.w = list(evs)
                    vw.b.rs = []
                yield
                for p0 in range(hb * 8, hb * 8 + 8, 2):
                    yield from pair_rr([scan_chunk_gen(seg, cc, p0, 0),
                                        scan_chunk_gen(seg, cc, p0 + 1, 1)])
            if done is not None:
                done[0] = cc + 1

    def wo_tc_gen(tc):
        bks = [ps_main.get() for _ in range(4)]
        for hh in range(H):
            wslot = hh % 2
            wap = wdb.ap()[:, wslot * 2048:(wslot + 1) * 2048]
            wbuf = wdb_b[wslot]
            kb.dma("sy", wap, wo.rearrange("(a p) d -> a p d", p=128)[hh],
                   wbufs=(wbuf,))
            oslot = hh % 2
            oap = ogl.ap()[:, oslot * 128:(oslot + 1) * 128]
            obuf = ogl_b[oslot]
            kb.dma("sy", oap, sp_og[hh, tc], rbufs=(spog_b[hh][tc],),
                   wbufs=(obuf,))
            for dsl in range(4):
                pb, pbuf = bks[dsl]
                te(lambda h, pb=pb, oap=oap, wap=wap, dsl=dsl, hh=hh: h.matmul(
                    pb.ap()[:, 0:512], oap,
                    wap[:, dsl * 512:(dsl + 1) * 512],
                    start=(hh == 0), stop=(hh == H - 1)),
                   reads=(pbuf.w,), rbufs=(obuf, wbuf),
                   wbufs=(pbuf,) if hh in (0, H - 1) else ())
            if hh % 2 == 1:
                yield
        xslot = tc % 2
        xap = xtm.ap()[:, xslot * D:(xslot + 1) * D]
        xbuf = xtm_b[xslot]
        kb.dma("sy", xap, x_own.rearrange("(a p) d -> a p d", p=128)[tc],
               wbufs=(xbuf,))
        for dsl in range(4):
            pb, pbuf = bks[dsl]
            ve(lambda h, pb=pb, xap=xap, dsl=dsl: h.tensor_tensor(
                xap[:, dsl * 512:(dsl + 1) * 512],
                xap[:, dsl * 512:(dsl + 1) * 512],
                pb.ap()[:, 0:512], ALU.add),
               reads=(xbuf.w,), rbufs=(pbuf,), wbufs=(xbuf,))
        yield

    def ffn_slice_gen(sl):
        for tci in range(NTF):
            tc = sl * NTF + tci
            par = tc % 2
            co = par * 8
            lcb = lncol_b[par]
            xap = xtm.ap()[:, par * D:(par + 1) * D]
            xbuf = xtm_b[par]
            ve(lambda h, xap=xap, co=co: h.tensor_scalar(
                xap[:, 0:D:2], xap[:, 0:D:2], 1.0, 0.0, ALU.mult, ALU.add,
                accum_out=lncol.ap()[:, co:co + 1]),
               wbufs=(lcb, xbuf))
            ve(lambda h, xap=xap, co=co: h.tensor_scalar(
                xap[:, 1:D:2], xap[:, 1:D:2], 1.0, 0.0, ALU.mult, ALU.add,
                accum_out=lncol.ap()[:, co + 4:co + 5]),
               reads=(lcb.w,), wbufs=(lcb, xbuf))
            se(lambda h, xap=xap, co=co: h.activation(
                lnd.ap(), xap[:, 0:D:2], AF.Square,
                accum_out=lncol.ap()[:, co + 1:co + 2]),
               reads=(lcb.w,), rbufs=(xbuf,), wbufs=(lcb, lnd_b))
            se(lambda h, xap=xap, co=co: h.activation(
                lnd.ap(), xap[:, 1:D:2], AF.Square,
                accum_out=lncol.ap()[:, co + 5:co + 6]),
               reads=(lcb.w,), rbufs=(xbuf,), wbufs=(lcb, lnd_b))
            ve(lambda h, co=co: h.tensor_tensor(lncol.ap()[:, co:co + 2],
                                                lncol.ap()[:, co:co + 2],
                                                lncol.ap()[:, co + 4:co + 6],
                                                ALU.add),
               rbufs=(lcb,), wbufs=(lcb,))
            ve(lambda h, co=co: h.tensor_scalar(lncol.ap()[:, co + 2:co + 4],
                                                lncol.ap()[:, co:co + 2],
                                                1.0 / D, None, ALU.mult),
               rbufs=(lcb,), wbufs=(lcb,))
            ve(lambda h, co=co: h.tensor_tensor(lncol.ap()[:, co + 4:co + 5],
                                                lncol.ap()[:, co + 2:co + 3],
                                                lncol.ap()[:, co + 2:co + 3],
                                                ALU.mult),
               rbufs=(lcb,), wbufs=(lcb,))
            ve(lambda h, co=co: h.tensor_tensor(lncol.ap()[:, co + 5:co + 6],
                                                lncol.ap()[:, co + 3:co + 4],
                                                lncol.ap()[:, co + 4:co + 5],
                                                ALU.subtract),
               rbufs=(lcb,), wbufs=(lcb,))
            se(lambda h, co=co: h.activation(lncol.ap()[:, co + 6:co + 7],
                                             lncol.ap()[:, co + 5:co + 6],
                                             AF.Ln, bias=c_eps5.ap()[:, 0:1]),
               rbufs=(lcb, cbuf), wbufs=(lcb,))
            se(lambda h, co=co: h.activation(lncol.ap()[:, co + 7:co + 8],
                                             lncol.ap()[:, co + 6:co + 7],
                                             AF.Exp, scale=-0.5),
               rbufs=(lcb,), wbufs=(lcb,))
            ve(lambda h, xap=xap, tci=tci: h.tensor_scalar(
                dacc[tci].ap(), xap, 1.0, None, ALU.mult),
               rbufs=(xbuf,), wbufs=(dacc[tci].b,))
            ve(lambda h, xap=xap, co=co: h.tensor_scalar(
                xap, xap, lncol.ap()[:, co + 2:co + 3],
                lncol.ap()[:, co + 7:co + 8], ALU.subtract, ALU.mult),
               reads=(xbuf.w,), rbufs=(lcb,), wbufs=(xbuf,))
            yield
            for dc in range(NDC):
                pb, pbuf = ps_main.get()
                te(lambda h, pb=pb, dc=dc, xap=xap: h.transpose(
                    pb.ap()[:, 0:128], xap[:, dc * 128:(dc + 1) * 128],
                    c_id.ap()),
                   rbufs=(xbuf, cbuf), wbufs=(pbuf,))
                dst = h2f[dc].ap(tci * 128, (tci + 1) * 128)
                se(lambda h, pb=pb, dst=dst: h.activation(
                    dst, pb.ap()[:, 0:128], AF.Copy),
                   rbufs=(pbuf,), wbufs=(h2f[dc].b,))
                if dc == NDC // 2 - 1:
                    yield
            yield
        for blk in range(16):
            for u in range(4):
                ip = blk * 4 + u
                for which, ic in ((0, ip), (1, ip + 64)):
                    wap = wdb.ap()[:, which * 2048:(which + 1) * 2048]
                    wbuf = wdb_b[which]
                    kb.dma("sy", wap, wgu[ic], wbufs=(wbuf,))
                    pb, pbuf = ps_main.get()
                    for dc in range(NDC):
                        te(lambda h, pb=pb, wap=wap, dc=dc: h.matmul(
                            pb.ap()[:, 0:SW_F], wap[:, dc * 128:(dc + 1) * 128],
                            h2f[dc].ap(),
                            start=(dc == 0), stop=(dc == NDC - 1)),
                           reads=(pbuf.w,), rbufs=(h2f[dc].b, wbuf),
                           wbufs=(pbuf,) if dc in (0, NDC - 1) else ())
                    gv = gup[which]
                    if which == 0:
                        se(lambda h, pb=pb, gv=gv: h.activation(
                            gv.ap(), pb.ap()[:, 0:SW_F], AF.Silu),
                           rbufs=(pbuf,), wbufs=(gv.b,))
                    else:
                        se(lambda h, pb=pb, gv=gv: h.activation(
                            gv.ap(), pb.ap()[:, 0:SW_F], AF.Copy),
                           rbufs=(pbuf,), wbufs=(gv.b,))
                    yield
                ve(lambda h, u=u: h.tensor_tensor(
                    fti[u].ap(), gup[0].ap(), gup[1].ap(), ALU.mult),
                   rbufs=(gup[0].b, gup[1].b), wbufs=(fti[u].b,))
                wap3 = wdb.ap()[:, 2 * 2048:3 * 2048]
                wbuf3 = wdb_b[2]
                kb.dma("sy", wap3,
                       wdn.rearrange("(a p) d -> a p d", p=128)[ip],
                       wbufs=(wbuf3,))
                for tci in range(NTF):
                    for dsl in range(4):
                        pb, pbuf = ps_main.get()
                        te(lambda h, pb=pb, u=u, tci=tci, dsl=dsl, wap3=wap3:
                           h.matmul(
                               pb.ap()[:, 0:512],
                               fti[u].ap(tci * 128, (tci + 1) * 128),
                               wap3[:, dsl * 512:(dsl + 1) * 512],
                               start=True, stop=True),
                           rbufs=(fti[u].b, wbuf3), wbufs=(pbuf,))
                        ve(lambda h, pb=pb, tci=tci, dsl=dsl: h.tensor_tensor(
                            dacc[tci].ap(dsl * 512, (dsl + 1) * 512),
                            dacc[tci].ap(dsl * 512, (dsl + 1) * 512),
                            pb.ap()[:, 0:512], ALU.add),
                           reads=(dacc[tci].b.w,), rbufs=(pbuf,),
                           wbufs=(dacc[tci].b,))
                yield
        for tci in range(NTF):
            tc = sl * NTF + tci
            ev = kb.dma("gp", out.rearrange("(a p) d -> a p d", p=128)[tc],
                        dacc[tci].ap(), rbufs=(dacc[tci].b,))
            final_evs.append(ev)
        yield

    final_evs = []
    own_done = [0]

    def wf_gen():
        stitch(h2f + dacc, [v.b for v in h1v])
        for sl in range(NSL_F):
            for tci in range(NTF):
                tc = sl * NTF + tci
                while own_done[0] < tc + 1:
                    yield
                yield from wo_tc_gen(tc)
            yield from ffn_slice_gen(sl)

    HC = NCC // 2
    run_units((produce_round_gen(0, 0, x_hist), 1))
    run_units((produce_round_gen(0, 1, x_hist), 1), (scan_chunks_gen(0, 0, HC), 2))
    run_units((produce_round_gen(1, 0, x_own), 1), (scan_chunks_gen(0, HC, NCC), 2))
    run_units((produce_round_gen(1, 1, x_own), 1), (scan_chunks_gen(1, 0, HC), 2))
    own_done[0] = HC
    run_units((wf_gen(), 2), (scan_chunks_gen(1, HC, NCC, own_done), 1))
    kb.finalize(final_evs)
    return nc


def kernel(L0_attn_hidden_states, L0_attn_Wq, L0_attn_Wk, L0_attn_Wv,
           L0_attn_conv_q, L0_attn_conv_k, L0_attn_conv_v, L0_attn_A_log,
           L0_attn_Wfa, L0_attn_Wfb, L0_attn_dt_bias, L0_attn_Wb,
           L0_attn_Wga, L0_attn_Wgb, L0_attn_norm_w, L0_attn_Wo,
           L0_ffn_w_gate_up, L0_ffn_w_down):
    X = np.ascontiguousarray(np.asarray(L0_attn_hidden_states, np.float32))
    B, T, _ = X.shape
    Thalf = T // 2
    consts = _consts()
    f32 = lambda a: np.ascontiguousarray(np.asarray(a, np.float32))
    wq_c = _wchunk(f32(L0_attn_Wq)).reshape(NDC, 128, NDC * 128)
    wk_c = _wchunk(f32(L0_attn_Wk)).reshape(NDC, 128, NDC * 128)
    wv_c = _wchunk(f32(L0_attn_Wv)).reshape(NDC, 128, NDC * 128)
    wgu_c = _wchunk(f32(np.asarray(L0_ffn_w_gate_up).T)).reshape(128, 128, NDC * 128)
    wdn_t = f32(np.asarray(L0_ffn_w_down).T)
    alog_rep = np.ascontiguousarray(
        np.broadcast_to(f32(L0_attn_A_log).reshape(1, H), (128, H)))

    nc = build_program(Thalf)
    common = dict(
        wq=wq_c, wk=wk_c, wv=wv_c, wo=f32(L0_attn_Wo),
        wfa=f32(L0_attn_Wfa), wfb=f32(L0_attn_Wfb), wb=f32(L0_attn_Wb),
        wga=f32(L0_attn_Wga), wgb=f32(L0_attn_Wgb), wgu=wgu_c, wdn=wdn_t,
        convq=f32(L0_attn_conv_q), convk=f32(L0_attn_conv_k),
        convv=f32(L0_attn_conv_v), dtb=f32(L0_attn_dt_bias), alog=alog_rep,
        normw=f32(L0_attn_norm_w),
    )
    for k, v in consts.items():
        common["c_" + k] = v
    zeros = np.zeros((Thalf, D), np.float32)
    in_maps = []
    for core in range(8):
        b, half = core // 2, core % 2
        m = dict(common)
        m["x_own"] = np.ascontiguousarray(X[b, half * Thalf:(half + 1) * Thalf])
        m["x_hist"] = (np.ascontiguousarray(X[b, 0:Thalf]) if half == 1
                       else zeros)
        in_maps.append(m)
    res = run_bass_kernel_spmd(nc, in_maps, core_ids=list(range(8)))
    outp = np.empty((B, T, D), np.float32)
    for core in range(8):
        b, half = core // 2, core % 2
        outp[b, half * Thalf:(half + 1) * Thalf] = res.results[core]["out"]
    return outp
